# version 09
# baseline (speedup 1.0000x reference)
import contextlib
import numpy as np
import ml_dtypes
import concourse.bass as bass
import concourse.mybir as mybir
from concourse.bass_utils import run_bass_kernel_spmd
from concourse.alu_op_type import AluOpType as OP

F32 = mybir.dt.float32
BF16 = mybir.dt.bfloat16
AF = mybir.ActivationFunctionType
bf = ml_dtypes.bfloat16

D = 4096
T = 8192
NPOS = 64
NSLOT = 8
NGRP = 8
GH = 4
NCHUNK = 32
SCALE = 1.0 / float(np.sqrt(128.0))
SHIFT = -10.0
PADBIAS = -30000.0
NU = 12
EPS = 1e-5


class Event:
    __slots__ = ("engine", "emit", "waits", "idx", "inc", "sem")

    def __init__(self, engine, emit, waits, inc, sem):
        self.engine = engine
        self.emit = emit
        self.waits = list(waits)
        self.idx = None
        self.inc = inc
        self.sem = sem


class Sched:
    ENGINES = ("tensor", "scalar", "vector", "gpsimd", "sync")

    def __init__(self):
        self.ev = {e: [] for e in self.ENGINES}
        self.sem_counts = {}
        self.final = False
        self._gate = {e: [] for e in self.ENGINES}

    def set_gate(self, events):
        evs = [e for e in events if e is not None]
        for eng in self.ENGINES:
            self._gate[eng] = list(evs)

    def event(self, engine, emit, waits=(), inc=1, sem=None):
        w = list(waits)
        if self._gate[engine]:
            w.extend(x for x in self._gate[engine]
                     if not (x.engine == engine and x.sem == engine))
            self._gate[engine] = []
        e = Event(engine, emit, w, inc, sem or engine)
        self.ev[engine].append(e)
        return e

    def last(self):
        seen = {}
        for eng in self.ENGINES:
            for e in self.ev[eng]:
                seen[e.sem] = e
        return list(seen.values())

    def finalize(self):
        order = []
        for eng in self.ENGINES:
            order.extend(self.ev[eng])
        counts = {}
        for eng in self.ENGINES:
            for e in self.ev[eng]:
                counts[e.sem] = counts.get(e.sem, 0) + e.inc
                e.idx = counts[e.sem]
        self.sem_counts = counts
        self.final = True
        return dict(counts)

    def body(self, engine, eng_obj, sems):
        assert self.final
        for e in self.ev[engine]:
            for w in e.waits:
                assert w.idx is not None
                eng_obj.wait_ge(sems[w.sem], w.idx)
            ins = e.emit(eng_obj)
            assert ins is not None, f"emit returned None on {engine}"
            ins.then_inc(sems[e.sem], e.inc)

    def check_deadlock(self):
        pc = {e: 0 for e in self.ev}
        counts = {}
        progress = True
        while progress:
            progress = False
            for eng, lst in self.ev.items():
                while pc[eng] < len(lst):
                    e = lst[pc[eng]]
                    if all(counts.get(w.sem, 0) >= w.idx for w in e.waits):
                        counts[e.sem] = counts.get(e.sem, 0) + e.inc
                        pc[eng] += 1
                        progress = True
                    else:
                        break
        stuck = {e: (pc[e], len(self.ev[e])) for e in self.ev
                 if pc[e] < len(self.ev[e])}
        return stuck or None

    def validate_dma_sems(self):
        return None


def core_layout(i):
    pos2log = [None] * 64
    for k in range(8):
        pos2log[8 * k + 7] = 8 * k + i
    for p in range(i):
        pos2log[p] = p
    for k in range(1, 8):
        vals = list(range(8 * (k - 1) + i + 1, 8 * k + i))
        assert len(vals) == 7
        for j, v in enumerate(vals):
            pos2log[8 * k + j] = v
    for k in range(8):
        seen = {pos2log[p] for p in range(8 * k + 8)} - {None}
        assert seen == set(range(8 * k + i + 1)), (i, k)
    pads = set(range(i, 7))
    return pos2log, pads


def batches_for(nj):
    return [(j, min(2, nj - j)) for j in range(0, nj, 2)]


def sgroups_for(nj):
    return [(j, min(6, nj - j)) for j in range(0, nj, 6)]


def add_attention(sched, alloc, palloc, kT, V, qT, rawT, tri01, biasv, bias10,
                  ones_sq, qready=None, filler=None):
    sps = [palloc(f"at_sps{i}", [128, 1024], F32) for i in range(2)]
    ppv = [palloc(f"at_ppv{i}", [128, 512], F32) for i in range(2)]
    psum_s = palloc("at_psum_s", [128, 512], F32)
    uring = alloc("at_uring", [128, NU * 512], BF16)
    sacc = [alloc(f"at_sacc{i}", [128, 512], F32) for i in range(2)]
    tp = [alloc(f"at_tp{i}", [128, 512], BF16) for i in range(3)]
    tq = alloc("at_tq", [128, 512], BF16)
    tr = alloc("at_tr", [128, 512], BF16)
    sden = [alloc(f"at_sden{i}", [128, 512], F32) for i in range(2)]
    sinvb = [alloc(f"at_sinvb{i}", [128, 512], F32) for i in range(2)]

    def qslice(g, k):
        off = (k * 32 + g * GH) * 128
        return qT[:, off:off + 512]

    def u_t(gj, n=1):
        s = (gj % NU) * 512
        return uring[:, s:s + n * 512]

    st = {"exp_events": [], "pair_ev_by_gj": [], "pv_ev_by_gj": [],
          "vprev": None, "gj": 0, "gb": 0}
    fin = {}

    def ev_t(emit, waits=()):
        return sched.event("tensor", emit, waits)

    def ev_a(emit, waits=()):
        return sched.event("scalar", emit, waits)

    def ev_v(emit, waits=()):
        w = list(waits)
        if st["vprev"] is not None:
            w.append(st["vprev"])
        e = sched.event("vector", emit, w)
        st["vprev"] = e
        return e

    def make_ssum(x, f):
        waits = [f["sg_last"]]
        if x - 1 in fin and "sevac" in fin[x - 1]:
            waits.append(fin[x - 1]["sevac"])
        f["ssum"] = ev_t(lambda t, x=x: t.matmul(
            psum_s[:], ones_sq[:], sacc[x % 2][:], start=True, stop=True), waits)

    def make_sevac(x, f):
        waits = [f["ssum"]]
        if x - 2 in fin and "recip" in fin[x - 2]:
            waits.append(fin[x - 2]["recip"])
        f["sevac"] = ev_a(lambda s, x=x: s.copy(sden[x % 2][:], psum_s[:]), waits)

    def make_recip(x, f):
        waits = [f["sevac"]]
        if x - 2 in fin and "raw" in fin[x - 2]:
            waits.append(fin[x - 2]["raw"])
        f["recip"] = ev_v(lambda v, x=x: v.reciprocal(
            sinvb[x % 2][:], sden[x % 2][:]), waits)

    def make_raw(x, f, g, k):
        waits = [f["recip"], f["p_last"]]
        off = (k * 32 + g * GH) * 128
        f["raw"] = ev_v(lambda v, off=off, x=x: v.tensor_tensor(
            rawT[:, off:off + 512], ppv[x % 2][:], sinvb[x % 2][:], OP.mult),
            waits)

    entries = [(g, k) for g in range(NGRP) for k in range(NSLOT)]
    for ei, (g, k) in enumerate(entries):
        if filler is not None:
            filler()
        nj = 8 * k + 8
        bl = batches_for(nj)
        nb = len(bl)
        sg = sgroups_for(nj)
        gj0 = st["gj"]
        gb0 = st["gb"]
        f = {}
        fin[ei] = f
        fprev = fin.get(ei - 1)
        exp_ev = [None] * nb
        s_ev = [None] * nb
        p_ev = [None] * nb
        mask_ev = [None]
        glast = {}
        for gi, (j0, glen) in enumerate(sg):
            glast.setdefault((j0 + glen) // 2 - 1, []).append(gi)
        first_acc = [True]

        def scores_event(bi):
            j0, b = bl[bi]
            gbi = gb0 + bi
            waits = []
            if gbi >= 2:
                waits.append(st["exp_events"][gbi - 2])
            if bi == 0 and k == 0 and qready is not None:
                waits.append(qready(g))

            def emit(t, j0=j0, b=b, bi=bi, g=g, k=k):
                for jj in range(j0, j0 + b):
                    ins = t.matmul(
                        sps[bi % 2][:, (jj - j0) * 512:(jj - j0 + 1) * 512],
                        kT[:, jj * 128:(jj + 1) * 128],
                        qslice(g, k), start=True, stop=True)
                return ins
            s_ev[bi] = ev_t(emit, waits)

        def exp_event(bi):
            j0, b = bl[bi]
            gj = gj0 + j0
            waits = [s_ev[bi]]
            if gj + b - 1 >= NU:
                waits.append(st["pair_ev_by_gj"][gj + b - 1 - NU])
                waits.append(st["pv_ev_by_gj"][gj + b - 1 - NU])

            def emit(s, j0=j0, b=b, bi=bi, gj=gj):
                if j0 < 8:
                    for jj in range(j0, j0 + b):
                        ins = s.activation(
                            u_t(gj + (jj - j0)),
                            sps[bi % 2][:, (jj - j0) * 512:(jj - j0 + 1) * 512],
                            AF.Exp, bias=biasv[:, jj:jj + 1], scale=SCALE)
                else:
                    ins = s.activation(u_t(gj, b), sps[bi % 2][:, 0:b * 512],
                                       AF.Exp, bias=bias10[:], scale=SCALE)
                return ins
            e = ev_a(emit, waits)
            exp_ev[bi] = e
            st["exp_events"].append(e)

        def pv_event(bi):
            j0, b = bl[bi]
            waits = [exp_ev[bi]]
            if bi == 0 and ei >= 2:
                waits.append(fin[ei - 2]["raw"])
            if j0 + b == nj:
                waits.append(mask_ev[0])

            def emit(t, j0=j0, b=b, ei=ei, nj=nj, gj0=gj0):
                for jj in range(j0, j0 + b):
                    ins = t.matmul(ppv[ei % 2][:], V[:, jj * 128:(jj + 1) * 128],
                                   u_t(gj0 + jj), start=(jj == 0),
                                   stop=(jj == nj - 1))
                return ins
            p_ev[bi] = ev_t(emit, waits)
            for jj in range(j0, j0 + b):
                st["pv_ev_by_gj"].append(p_ev[bi])

        def tree_group(gi):
            j0, glen = sg[gi]
            npair = glen // 2
            if gi == len(sg) - 1:
                gd = gj0 + nj - 1
                mask_ev[0] = ev_v(lambda v, gd=gd: v.tensor_tensor(
                    u_t(gd), u_t(gd), tri01[:], OP.mult), [exp_ev[nb - 1]])
            pair_evs = []
            for pi in range(npair):
                jj = j0 + 2 * pi
                e = ev_v(lambda v, pi=pi, jj=jj, gj0=gj0: v.tensor_tensor(
                    tp[pi][:], u_t(gj0 + jj), u_t(gj0 + jj + 1), OP.add),
                    [exp_ev[jj // 2]])
                pair_evs.append(e)
                st["pair_ev_by_gj"].append(e)
                st["pair_ev_by_gj"].append(e)
            if npair == 3:
                ev_v(lambda v: v.tensor_tensor(tq[:], tp[0][:], tp[1][:], OP.add))
                ev_v(lambda v: v.tensor_tensor(tr[:], tq[:], tp[2][:], OP.add))
                red = tr
            elif npair == 2:
                ev_v(lambda v: v.tensor_tensor(tq[:], tp[0][:], tp[1][:], OP.add))
                red = tq
            else:
                red = tp[0]
            if first_acc[0]:
                w = []
                if ei - 2 in fin and "ssum" in fin[ei - 2]:
                    w.append(fin[ei - 2]["ssum"])
                ev_v(lambda v, red=red, ei=ei: v.tensor_scalar(
                    sacc[ei % 2][:], red[:], 0.0, None, OP.add), w)
                first_acc[0] = False
            else:
                ev_v(lambda v, red=red, ei=ei: v.tensor_tensor(
                    sacc[ei % 2][:], sacc[ei % 2][:], red[:], OP.add), [])
            f["sg_last"] = st["vprev"]

        for b in range(nb):
            scores_event(b)
            exp_event(b)
            if fprev:
                if b == 0:
                    make_ssum(ei - 1, fprev)
                    make_sevac(ei - 1, fprev)
                elif b == 1:
                    make_recip(ei - 1, fprev)
                elif b == 2:
                    make_raw(ei - 1, fprev, *entries[ei - 1])
            if b >= 2:
                pv_event(b - 2)
            for gi in glast.get(b, []):
                tree_group(gi)
        pv_event(nb - 2)
        pv_event(nb - 1)
        f["p_last"] = p_ev[nb - 1]
        assert len(st["pv_ev_by_gj"]) == gj0 + nj
        assert len(st["pair_ev_by_gj"]) == gj0 + nj
        st["gj"] += nj
        st["gb"] += nb

    last = len(entries) - 1
    f = fin[last]
    g, k = entries[last]
    make_ssum(last, f)
    make_sevac(last, f)
    make_recip(last, f)
    make_raw(last, f, g, k)
    return fin


def build_nc():
    nc = bass.Bass("TRN2", target_bir_lowering=False)
    xkv = nc.dram_tensor("xkv", [T, D], F32, kind="ExternalInput").ap()
    wq = nc.dram_tensor("wq", [D, D], F32, kind="ExternalInput").ap()
    wo = nc.dram_tensor("wo", [D, D], F32, kind="ExternalInput").ap()
    wbT = nc.dram_tensor("wbT", [D + 128, D], F32, kind="ExternalInput").ap()
    wkv = nc.dram_tensor("wkv", [D, 256], F32, kind="ExternalInput").ap()
    aflat = nc.dram_tensor("aflat", [D, 128], F32, kind="ExternalInput").ap()
    bflat = nc.dram_tensor("bflat", [128, D], F32, kind="ExternalInput").ap()
    wrt = nc.dram_tensor("wrt", [D, 128], F32, kind="ExternalInput").ap()
    ident_in = nc.dram_tensor("identc", [128, 128], BF16, kind="ExternalInput").ap()
    tri_in = nc.dram_tensor("tric", [128, 512], BF16, kind="ExternalInput").ap()
    biasv_in = nc.dram_tensor("biasvc", [128, 8], F32, kind="ExternalInput").ap()
    e2_in = nc.dram_tensor("e2c", [128, 128], BF16, kind="ExternalInput").ap()
    out_d = nc.dram_tensor("out", [1024, D], F32, kind="ExternalOutput").ap()
    x2scr = nc.dram_tensor("x2scr", [1024, D], F32).ap()

    sched = Sched()
    left = contextlib.ExitStack()

    def la(name, shape, dtype):
        return left.enter_context(nc.sbuf_tensor(name, shape, dtype, side="left"))

    with left:
        ident = la("identsb", [128, 128], BF16)
        tri01 = la("tri01", [128, 512], BF16)
        biasv = la("biasvsb", [128, 8], F32)
        e2sb = la("e2sb", [128, 128], BF16)
        bias10 = la("bias10", [128, 1], F32)
        eps_t = la("epst", [128, 1], F32)
        ones_sq = la("ones_sq", [128, 128], F32)
        identf = la("identf", [128, 128], F32)
        onechunk = la("onechunk", [128, 128], BF16)
        xnT = la("xnT", [128, NCHUNK * 1024], BF16)
        kvscope = contextlib.ExitStack()
        kT = kvscope.enter_context(nc.sbuf_tensor("kTs", [128, NPOS * 128], BF16, side="left"))
        V = kvscope.enter_context(nc.sbuf_tensor("Vs", [128, NPOS * 128], BF16, side="left"))

        sched.event("sync", lambda s: s.dma_start(ident[:], ident_in), inc=16, sem="c_setup")
        sched.event("sync", lambda s: s.dma_start(tri01[:], tri_in), inc=16, sem="c_setup")
        sched.event("sync", lambda s: s.dma_start(biasv[:], biasv_in), inc=16, sem="c_setup")
        ev_setup = sched.event("sync", lambda s: s.dma_start(e2sb[:], e2_in), inc=16, sem="c_setup")
        ev_mb = sched.event("gpsimd", lambda g: g.memset(bias10[:], SHIFT))
        ev_me = sched.event("gpsimd", lambda g: g.memset(eps_t[:], EPS))
        ev_mo = sched.event("gpsimd", lambda g: g.memset(ones_sq[:], 1.0))
        ev_idf = sched.event("scalar", lambda s: s.copy(identf[:], ident[:]),
                             waits=[ev_setup])
        ev_oc0 = sched.event("gpsimd", lambda g: g.memset(onechunk[:], 0.0))
        ev_oc = sched.event("gpsimd", lambda g: g.memset(onechunk[0:1, :], 1.0),
                            waits=[ev_oc0])
        consts = [ev_setup, ev_mb, ev_me, ev_mo, ev_idf, ev_oc]
        sched.set_gate(consts)

        p1 = contextlib.ExitStack()

        def r1(name, shape, dtype):
            return p1.enter_context(nc.sbuf_tensor(name, shape, dtype, side="right"))

        xst = [r1(f"xst{i}", [128, D], F32) for i in range(3)]
        xn_bf = [r1(f"xn_bf{i}", [128, D], BF16) for i in range(2)]
        wkvbf = r1("wkvbf", [128, NCHUNK, 256], BF16)
        xnTq = r1("xnTq", [128, NCHUNK * 256], BF16)
        stats = r1("stats", [128, 48], F32)
        mv = [r1(f"mv{i}", [128, 2], F32) for i in range(4)]
        rstd = [r1(f"rstd{i}", [128, 1], F32) for i in range(4)]
        negmur = [r1(f"negmur{i}", [128, 1], F32) for i in range(4)]
        lnv = [r1(f"lnv{i}", [128, 1], F32) for i in range(4)]

        p1ps = contextlib.ExitStack()
        ptx = p1ps.enter_context(nc.psum_tensor("ptx", [128, D], BF16))
        kps = [p1ps.enter_context(nc.psum_tensor(f"kps{i}", [128, 256], F32)) for i in range(2)]
        vps = [p1ps.enter_context(nc.psum_tensor(f"vps{i}", [128, 128], F32)) for i in range(2)]

        ev_wkv = sched.event(
            "gpsimd",
            lambda g: g.dma_start(wkvbf[:], wkv.rearrange("(c p) h -> p c h", p=128)),
            waits=consts, inc=16, sem="c_wkv")

        ev_x = [None] * NPOS
        ev_apply = [None] * NPOS
        ev_applyA = [None] * NPOS
        ev_tx = [None] * NPOS
        ev_txevac = [None] * NPOS
        ev_kv = [None] * (NPOS // 2)
        ev_kvevac = [None] * (NPOS // 2)
        ev_vmm = [None] * NPOS
        ev_vev = [None] * NPOS
        ev_ag_prev = [None]
        ev_ag_l = [None] * NPOS
        ev_nm_l = [None] * NPOS

        def stageA(p):
            q4, r4 = p // 2, p % 2
            w = [ev_setup] if p < 3 else [ev_apply[p - 3], ev_applyA[p - 3]]
            ev_x[p] = sched.event(
                "sync",
                lambda s, p=p: s.dma_start(xst[p % 3][:], xkv[p * 128:(p + 1) * 128, :]),
                waits=w, inc=16, sem=f"c_x{p % 3}")

            def em_stats(v, p=p):
                for c in range(8):
                    ins = v.bn_stats(stats[:, c * 6:(c + 1) * 6],
                                     xst[p % 3][:, c * 512:(c + 1) * 512])
                return ins
            w = [ev_x[p]]
            if ev_ag_prev[0] is not None:
                w.append(ev_ag_prev[0])
            ev_st = sched.event("vector", em_stats, waits=w)
            ev_ag = sched.event("vector",
                                lambda v, p=p: v.bn_aggr(mv[p % 4][:], stats[:]),
                                waits=[ev_st])
            ev_ag_prev[0] = ev_ag
            ev_ag_l[p] = ev_ag
            wln = [ev_ag]
            if p >= 4:
                wln.append(ev_apply[p - 4])
            ev_ln = sched.event("scalar", lambda s, p=p: s.activation(
                lnv[p % 4][:], mv[p % 4][:, 1:2], AF.Ln, bias=eps_t[:], scale=1.0),
                waits=wln)
            ev_rs = sched.event("scalar", lambda s, p=p: s.activation(
                rstd[p % 4][:], lnv[p % 4][:], AF.Exp, bias=0.0, scale=-0.5),
                waits=[ev_ln])
            ev_nm_l[p] = sched.event("vector", lambda v, p=p: v.tensor_scalar(
                negmur[p % 4][:], mv[p % 4][:, 0:1], rstd[p % 4][:], -1.0,
                OP.mult, OP.mult), waits=[ev_rs])

        def stageB(p):
            q4, r4 = p // 2, p % 2
            wap = [ev_nm_l[p]]
            if p >= 2:
                wap.append(ev_tx[p - 2])
            ev_apA = sched.event("scalar", lambda s, p=p: s.activation(
                xn_bf[p % 2][:, 0:2048], xst[p % 3][:, 0:2048], AF.Identity,
                bias=negmur[p % 4][:], scale=rstd[p % 4][:]), waits=wap)
            ev_apB = sched.event("vector", lambda v, p=p: v.tensor_scalar(
                xn_bf[p % 2][:, 2048:4096], xst[p % 3][:, 2048:4096],
                rstd[p % 4][:], negmur[p % 4][:], OP.mult, OP.add), waits=wap)
            ev_apply[p] = ev_apB
            ev_applyA[p] = ev_apA
            wtx = [ev_apA, ev_apB]
            if p >= 1:
                wtx.append(ev_txevac[p - 1][0])
                wtx.append(ev_txevac[p - 1][1])

            def em_tx(t, p=p):
                for c in range(NCHUNK):
                    ins = t.transpose(ptx[:, c * 128:(c + 1) * 128],
                                      xn_bf[p % 2][:, c * 128:(c + 1) * 128], ident[:])
                return ins
            ev_tx[p] = sched.event("tensor", em_tx, waits=wtx)

            wtev = [ev_tx[p]]
            if q4 >= 1:
                wtev.append(ev_kv[q4 - 1])
            if p >= 2:
                wtev.append(ev_vmm[p - 2])

            def em_teA(s, r4=r4):
                return s.copy(
                    xnTq[:].rearrange("q (c n) -> q c n", c=NCHUNK)[:, 0:16, r4 * 128:(r4 + 1) * 128],
                    ptx[:].rearrange("q (c n) -> q c n", c=NCHUNK)[:, 0:16, :])

            def em_teB(s, r4=r4):
                return s.copy(
                    xnTq[:].rearrange("q (c n) -> q c n", c=NCHUNK)[:, 16:32, r4 * 128:(r4 + 1) * 128],
                    ptx[:].rearrange("q (c n) -> q c n", c=NCHUNK)[:, 16:32, :])
            ev_teA = sched.event("scalar", em_teA, waits=wtev)
            ev_teB = sched.event("scalar", em_teB, waits=[ev_teA])
            if p % 8 == 7:
                kslot = p // 8

                def em_spA(s, kslot=kslot):
                    return s.copy(
                        xnT[:].rearrange("q (c n) -> q c n", c=NCHUNK)[:, 0:16, kslot * 128:(kslot + 1) * 128],
                        ptx[:].rearrange("q (c n) -> q c n", c=NCHUNK)[:, 0:16, :])

                def em_spB(v, kslot=kslot):
                    return v.tensor_copy(
                        xnT[:].rearrange("q (c n) -> q c n", c=NCHUNK)[:, 16:32, kslot * 128:(kslot + 1) * 128],
                        ptx[:].rearrange("q (c n) -> q c n", c=NCHUNK)[:, 16:32, :])
                ev_teA = sched.event("scalar", em_spA, waits=[ev_teA])
                ev_teB = sched.event("vector", em_spB, waits=[ev_teB])
            ev_txevac[p] = (ev_teA, ev_teB)

        def stageB2(p):
            q4, r4 = p // 2, p % 2
            ev_teA, ev_teB = ev_txevac[p]
            wv_ = [ev_teA, ev_teB, ev_wkv]
            if p >= 2:
                wv_.append(ev_vev[p - 2])

            def em_v(t, p=p, r4=r4):
                for c in range(NCHUNK):
                    ins = t.matmul(vps[p % 2][:],
                                   xnTq[:, c * 256 + r4 * 128:c * 256 + (r4 + 1) * 128],
                                   wkvbf[:, c:c + 1, 128:256].rearrange('p a n -> p (a n)'),
                                   start=(c == 0), stop=(c == NCHUNK - 1))
                return ins
            ev_vmm[p] = sched.event("tensor", em_v, waits=wv_)

            if r4 == 1:
                wkvq = [e for j in range(2) for e in ev_txevac[2 * q4 + j]] + [ev_wkv]
                if q4 >= 2:
                    wkvq.append(ev_kvevac[q4 - 2])

                def em_kv(t, q4=q4):
                    for c in range(NCHUNK):
                        ins = t.matmul(kps[q4 % 2][:], wkvbf[:, c:c + 1, 0:128].rearrange('p a n -> p (a n)'),
                                       xnTq[:, c * 256:(c + 1) * 256],
                                       start=(c == 0), stop=(c == NCHUNK - 1))
                    return ins
                ev_kv[q4] = sched.event("tensor", em_kv, waits=wkvq)

        def stageB3(p):
            q4, r4 = p // 2, p % 2
            ev_vev[p] = sched.event(
                "scalar", lambda s, p=p: s.copy(V[:, p * 128:(p + 1) * 128], vps[p % 2][:]),
                waits=[ev_vmm[p]])
            if r4 == 1:
                def em_kvev(s, q4=q4):
                    return s.copy(kT[:, q4 * 256:(q4 + 1) * 256], kps[q4 % 2][:])
                ev_kvevac[q4] = sched.event("scalar", em_kvev, waits=[ev_kv[q4]])

        for p in range(NPOS):
            stageA(p)
            if p >= 4:
                stageB3(p - 4)
            if p >= 2:
                stageB2(p - 2)
            if p >= 1:
                stageB(p - 1)
        stageB3(NPOS - 4)
        stageB2(NPOS - 2)
        stageB3(NPOS - 3)
        stageB(NPOS - 1)
        stageB2(NPOS - 1)
        stageB3(NPOS - 2)
        stageB3(NPOS - 1)

        p1_tail = sched.last()
        p1.close()
        p1ps.close()
        qscope = contextlib.ExitStack()
        qT = qscope.enter_context(
            nc.sbuf_tensor("qT", [128, NSLOT * 32 * 128], BF16, side="right"))

        p2 = contextlib.ExitStack()
        wqbf = [p2.enter_context(nc.sbuf_tensor(f"wqbf{i}", [128, NCHUNK, 128], BF16, side="right")) for i in range(2)]
        p2ps = contextlib.ExitStack()
        qps = p2ps.enter_context(nc.psum_tensor("qps", [128, 512], F32))

        sched.set_gate(p1_tail)
        ev_wq = [None] * 32
        ev_qmm = [None] * 32
        ev_qev = [None] * 32
        _qstate = {"half": 0, "prev_evac": None}

        def q_emit_half():
            hid = _qstate["half"]
            if hid >= 64:
                return
            _qstate["half"] = hid + 1
            ot, hh = hid // 2, hid % 2
            if hh == 0:
                w = [] if ot < 2 else [ev_qmm[ot - 2]]
                ev_wq[ot] = sched.event(
                    "gpsimd",
                    lambda g, ot=ot: g.dma_start(
                        wqbf[ot % 2][:],
                        wq.rearrange("(c p) o -> p c o", p=128)[:, :, ot * 128:(ot + 1) * 128]),
                    waits=w, inc=16, sem=f"c_wq{ot % 2}")

            def em_q(t, ot=ot, hh=hh):
                for c in range(NCHUNK):
                    ins = t.matmul(qps[:], wqbf[ot % 2][:, c:c + 1, :].rearrange('p a n -> p (a n)'),
                                   xnT[:, c * 1024 + hh * 512:c * 1024 + (hh + 1) * 512],
                                   start=(c == 0), stop=(c == NCHUNK - 1))
                return ins
            w = [ev_wq[ot]]
            if _qstate["prev_evac"] is not None:
                w.append(_qstate["prev_evac"])
            ev_m = sched.event("tensor", em_q, waits=w)

            def em_qev(s, ot=ot, hh=hh):
                q3 = qT[:].rearrange("p (k n) -> p k n", k=8)
                return s.copy(q3[:, hh * 4:(hh + 1) * 4, ot * 128:(ot + 1) * 128],
                              qps[:].rearrange("p (k n) -> p k n", k=4))
            ev_e = sched.event("scalar", em_qev, waits=[ev_m])
            _qstate["prev_evac"] = ev_e
            if hh == 1:
                ev_qmm[ot] = ev_m
                ev_qev[ot] = ev_e

        for _ in range(16):
            q_emit_half()
        p3 = contextlib.ExitStack()
        p3ps = contextlib.ExitStack()
        def qready(g):
            e = ev_qev[4 * g + 3]
            assert e is not None, f"q outtile {4 * g + 3} not yet emitted"
            return e
        add_attention(
            sched,
            lambda n, s, d: p3.enter_context(nc.sbuf_tensor(n, s, d, side="right")),
            lambda n, s, d: p3ps.enter_context(nc.psum_tensor(n, s, d)),
            kT, V, qT, qT, tri01, biasv, bias10, ones_sq, qready=qready,
            filler=q_emit_half)
        p3_tail = sched.last()
        p3.close()
        p3ps.close()
        p2.close()
        p2ps.close()
        kvscope.close()

        p4 = contextlib.ExitStack()
        ws4 = [xnT[:, i * 16384:(i + 1) * 16384].rearrange("p (q o) -> p q o", q=32)
               for i in range(2)]
        xsst = [p4.enter_context(nc.sbuf_tensor(f"xsst{i}", [128, 512], F32, side="right")) for i in range(2)]
        x2st = [p4.enter_context(nc.sbuf_tensor(f"x2st{i}", [128, 512], F32, side="right")) for i in range(2)]
        p4ps = contextlib.ExitStack()
        NB4 = 4
        yps = [p4ps.enter_context(nc.psum_tensor(f"yps{i}", [128, 512], F32)) for i in range(NB4)]

        sched.set_gate(p3_tail)
        rawT4 = qT[:].rearrange("p (k n) -> p k n", k=8)
        wor = wo.rearrange("(q p) o -> p q o", p=128)
        ev_w4 = {}
        ev_ymm = {}
        ev_yev = {}
        ev_xs = {}
        ev_x2w = {}
        p3t_tensor = [e for e in p3_tail if e.sem == "tensor"]
        for os_ in range(8):
            w = []
            if os_ >= 2:
                w.append(ev_ymm[(os_ - 2, 7)])
            else:
                w.extend(p3t_tensor)
            ev_w4[os_] = sched.event(
                "gpsimd",
                lambda g, os_=os_: g.dma_start(
                    ws4[os_ % 2], wor[:, :, os_ * 512:(os_ + 1) * 512]),
                waits=w, inc=16, sem=f"c_wo{os_ % 2}")
            for tt in range(8):
                j = os_ * 8 + tt
                w = [ev_w4[os_]]
                if j >= NB4:
                    w.append(ev_yev[((j - NB4) // 8, (j - NB4) % 8)])

                def em_y(t, os_=os_, tt=tt, j=j):
                    for ic in range(32):
                        ins = t.matmul(
                            yps[j % NB4][:],
                            rawT4[:, tt, ic * 128:(ic + 1) * 128],
                            xnT[:, (os_ % 2) * 16384 + ic * 512:(os_ % 2) * 16384 + (ic + 1) * 512],
                            start=(ic == 0), stop=(ic == 31))
                    return ins
                ev_ymm[(os_, tt)] = sched.event("tensor", em_y, waits=w)
                w = []
                if j >= 2:
                    w.append(ev_yev[((j - 2) // 8, (j - 2) % 8)])
                ev_xs[(os_, tt)] = sched.event(
                    "sync", lambda s, os_=os_, tt=tt: s.dma_start(
                        xsst[tt % 2][:],
                        xkv[(8 * tt + 7) * 128:(8 * tt + 8) * 128,
                            os_ * 512:(os_ + 1) * 512]),
                    waits=w, inc=16, sem=f"c_xs{tt % 2}")
                w = [ev_ymm[(os_, tt)], ev_xs[(os_, tt)]]
                if j >= 2:
                    w.append(ev_x2w[((j - 2) // 8, (j - 2) % 8)])
                ev_yev[(os_, tt)] = sched.event(
                    "vector", lambda v, os_=os_, tt=tt, j=j: v.tensor_tensor(
                        x2st[tt % 2][:], yps[j % NB4][:], xsst[tt % 2][:], OP.add),
                    waits=w)
                ev_x2w[(os_, tt)] = sched.event(
                    "sync", lambda s, os_=os_, tt=tt: s.dma_start(
                        x2scr[tt * 128:(tt + 1) * 128, os_ * 512:(os_ + 1) * 512],
                        x2st[tt % 2][:]),
                    waits=[ev_yev[(os_, tt)]], inc=16, sem=f"c_x2w{tt % 2}")
        p4_tail = sched.last()
        p4.close()
        p4ps.close()
        qscope.close()
        xn2T = xnT

        p45 = contextlib.ExitStack()

        def r45(name, shape, dtype):
            return p45.enter_context(nc.sbuf_tensor(name, shape, dtype, side="right"))
        x2rd = [r45(f"x2rd{i}", [128, D], F32) for i in range(2)]
        xn2bf = [r45(f"xn2bf{i}", [128, D], BF16) for i in range(2)]
        st2 = r45("st2", [128, 48], F32)
        mv2 = [r45(f"mv2{i}", [128, 2], F32) for i in range(2)]
        rstd2 = [r45(f"rstd2{i}", [128, 1], F32) for i in range(2)]
        negmur2 = [r45(f"negmur2{i}", [128, 1], F32) for i in range(2)]
        lnv2 = [r45(f"lnv2{i}", [128, 1], F32) for i in range(2)]
        p45ps = contextlib.ExitStack()
        ptx2 = [p45ps.enter_context(nc.psum_tensor(f"ptx2{i}", [128, D], BF16)) for i in range(2)]

        sched.set_gate(p4_tail)
        ev45 = []
        apl = {}
        txl = {}
        evacl = {}
        agp = None
        for tt in range(8):
            w = [] if tt < 2 else [apl[tt - 2]]
            ev_r = sched.event("sync", lambda s, tt=tt: s.dma_start(
                x2rd[tt % 2][:], x2scr[tt * 128:(tt + 1) * 128, :]), waits=w,
                inc=16, sem=f"c_x2rd{tt % 2}")

            def em_st2(v, tt=tt):
                for c in range(8):
                    ins = v.bn_stats(st2[:, c * 6:(c + 1) * 6],
                                     x2rd[tt % 2][:, c * 512:(c + 1) * 512])
                return ins
            ev_s = sched.event("vector", em_st2, waits=[ev_r] + ([agp] if agp else []))
            ev_a2 = sched.event("vector", lambda v, tt=tt: v.bn_aggr(mv2[tt % 2][:], st2[:]),
                                waits=[ev_s])
            agp = ev_a2
            wln = [ev_a2] + ([apl[tt - 2]] if tt >= 2 else [])
            ev_l = sched.event("scalar", lambda s, tt=tt: s.activation(
                lnv2[tt % 2][:], mv2[tt % 2][:, 1:2], AF.Ln, bias=eps_t[:], scale=1.0),
                waits=wln)
            ev_rs2 = sched.event("scalar", lambda s, tt=tt: s.activation(
                rstd2[tt % 2][:], lnv2[tt % 2][:], AF.Exp, bias=0.0, scale=-0.5),
                waits=[ev_l])
            ev_n = sched.event("vector", lambda v, tt=tt: v.tensor_scalar(
                negmur2[tt % 2][:], mv2[tt % 2][:, 0:1], rstd2[tt % 2][:], -1.0,
                OP.mult, OP.mult), waits=[ev_rs2])
            wap = [ev_n] + ([txl[tt - 2]] if tt >= 2 else [])
            ev_ap = sched.event("scalar", lambda s, tt=tt: s.activation(
                xn2bf[tt % 2][:], x2rd[tt % 2][:], AF.Identity,
                bias=negmur2[tt % 2][:], scale=rstd2[tt % 2][:]), waits=wap)
            apl[tt] = ev_ap

            def em_tx2(t, tt=tt):
                for c in range(NCHUNK):
                    ins = t.transpose(ptx2[tt % 2][:, c * 128:(c + 1) * 128],
                                      xn2bf[tt % 2][:, c * 128:(c + 1) * 128], ident[:])
                return ins
            wtx = [ev_ap] + ([evacl[tt - 2][0], evacl[tt - 2][1]] if tt >= 2 else [])
            ev_t45 = sched.event("tensor", em_tx2, waits=wtx)
            txl[tt] = ev_t45

            def em_e2A(s, tt=tt):
                return s.copy(
                    xn2T[:].rearrange("q (c n) -> q c n", c=NCHUNK)[:, 0:16, tt * 128:(tt + 1) * 128],
                    ptx2[tt % 2][:].rearrange("q (c n) -> q c n", c=NCHUNK)[:, 0:16, :])

            def em_e2B(v, tt=tt):
                return v.tensor_copy(
                    xn2T[:].rearrange("q (c n) -> q c n", c=NCHUNK)[:, 16:32, tt * 128:(tt + 1) * 128],
                    ptx2[tt % 2][:].rearrange("q (c n) -> q c n", c=NCHUNK)[:, 16:32, :])
            wal = [ev_t45] + ([ev_ymm[(7, 7)]] if tt == 0 else [])
            eA = sched.event("scalar", em_e2A, waits=wal)
            eB = sched.event("vector", em_e2B, waits=wal)
            evacl[tt] = (eA, eB)
            ev45.extend([eA, eB])
        p45_tail = sched.last()
        p45.close()
        p45ps.close()

        p5 = contextlib.ExitStack()
        p5m = contextlib.ExitStack()

        def a5(name, shape, dtype):
            return p5.enter_context(nc.sbuf_tensor(name, shape, dtype, side="right"))

        def a5m(name, shape, dtype):
            return p5m.enter_context(nc.sbuf_tensor(name, shape, dtype, side="right"))
        bbf = a5("bbf", [128, D], BF16)
        h_wT = a5("h_wT", [128, 1024], BF16)
        abf = a5m("abf", [128, NCHUNK, 128], BF16)
        wrbf = a5m("wrbf", [128, NCHUNK, 128], BF16)
        lgT = a5m("lgT", [128, 1024], F32)
        lg = a5m("lg", [128, 8], F32)
        m1 = a5m("m1", [128, 1], F32)
        negm1 = a5m("negm1", [128, 1], F32)
        ee = a5m("ee", [128, 8], F32)
        mask1 = a5m("mask1", [128, 8], F32)
        lg2 = a5m("lg2", [128, 8], F32)
        m2 = a5m("m2", [128, 1], F32)
        mask2 = a5m("mask2", [128, 8], F32)
        sel = a5m("sel", [128, 8], F32)
        es = a5m("es", [128, 8], F32)
        ssum5 = a5m("ssum5", [128, 1], F32)
        srec = a5m("srec", [128, 1], F32)
        ww = a5m("ww", [128, 128], F32)
        wTp = a5m("wTp", [128, 1024], BF16)
        wexp = a5m("wexp", [128, 1024], BF16)
        h_sb = a5m("h_sb", [128, 1024], BF16)

        p5ps1 = contextlib.ExitStack()
        plg = [p5ps1.enter_context(nc.psum_tensor(f"plg{i}", [128, 512], F32)) for i in range(2)]
        ptr5 = p5ps1.enter_context(nc.psum_tensor("ptr5", [128, 128], F32))
        ptwf = p5ps1.enter_context(nc.psum_tensor("ptwf", [128, 128], F32))
        pwe = [p5ps1.enter_context(nc.psum_tensor(f"pwe{i}", [128, 512], F32)) for i in range(2)]
        ph = [p5ps1.enter_context(nc.psum_tensor(f"ph{i}", [128, 512], F32)) for i in range(2)]

        sched.set_gate(p45_tail)
        ev_ab = sched.event("gpsimd", lambda g: g.dma_start(
            abf[:], aflat.rearrange("(c p) e -> p c e", p=128)), inc=16, sem="c_la")
        ev_bb = sched.event("gpsimd", lambda g: g.dma_start(bbf[:], bflat),
                            inc=16, sem="c_lb")
        ev_wr = sched.event("gpsimd", lambda g: g.dma_start(
            wrbf[:], wrt.rearrange("(c p) e -> p c e", p=128)), inc=16, sem="c_lr")
        ev_wwz = sched.event("gpsimd", lambda g: g.memset(ww[:], 0.0))

        def em_lg(t):
            for c in range(NCHUNK):
                t.matmul(plg[0][:], wrbf[:, c:c + 1, :].rearrange('p a n -> p (a n)'), xn2T[:, c * 1024:c * 1024 + 512],
                         start=(c == 0), stop=(c == NCHUNK - 1))
                ins = t.matmul(plg[1][:], wrbf[:, c:c + 1, :].rearrange('p a n -> p (a n)'),
                               xn2T[:, c * 1024 + 512:(c + 1) * 1024],
                               start=(c == 0), stop=(c == NCHUNK - 1))
            return ins
        ev_lgm = sched.event("tensor", em_lg, waits=[ev_wr] + ev45)

        def em_lge(s):
            s.copy(lgT[:, 0:512], plg[0][:])
            return s.copy(lgT[:, 512:1024], plg[1][:])
        ev_lge = sched.event("scalar", em_lge, waits=[ev_lgm])

        vch = [None]

        def ev5v(emit, waits=()):
            w = list(waits)
            if vch[0] is not None:
                w.append(vch[0])
            e = sched.event("vector", emit, w)
            vch[0] = e
            return e

        ev_wtev = [None] * 8
        prev_lgx = None
        prev_wtr = None
        for tt in range(8):
            w = [ev_lge]
            if prev_lgx is not None:
                w.append(prev_lgx)
            ev_ltr = sched.event("tensor", lambda t, tt=tt: t.transpose(
                ptr5[:], lgT[:, tt * 128:(tt + 1) * 128], identf[:]), waits=w)
            prev_lgx = sched.event("scalar", lambda s: s.copy(lg[:], ptr5[:, 0:8]),
                                   waits=[ev_ltr])
            ev5v(lambda v: v.reduce_max(m1[:], lg[:], axis=mybir.AxisListType.X), [prev_lgx])
            ev5v(lambda v: v.tensor_scalar(negm1[:], m1[:], -1.0, None, OP.mult))
            ev5v(lambda v: v.tensor_scalar(mask1[:], lg[:], m1[:], None, OP.is_equal))
            ev5v(lambda v: v.scalar_tensor_tensor(lg2[:], mask1[:], -1e30, lg[:], OP.mult, OP.add))
            ev5v(lambda v: v.reduce_max(m2[:], lg2[:], axis=mybir.AxisListType.X))
            ev5v(lambda v: v.tensor_scalar(mask2[:], lg2[:], m2[:], None, OP.is_equal))
            e7 = ev5v(lambda v: v.tensor_add(sel[:], mask1[:], mask2[:]))
            ev_ee = sched.event("scalar", lambda s: s.activation(
                ee[:], lg[:], AF.Exp, bias=negm1[:], scale=1.0), waits=[e7])
            ev5v(lambda v: v.tensor_mul(es[:], ee[:], sel[:]), [ev_ee])
            ev5v(lambda v: v.reduce_sum(ssum5[:], es[:], axis=mybir.AxisListType.X))
            ev5v(lambda v: v.reciprocal(srec[:], ssum5[:]))
            w = [ev_wwz]
            if prev_wtr is not None:
                w.append(prev_wtr)
            e11 = ev5v(lambda v: v.tensor_scalar(ww[:, 0:8], es[:], srec[:], None, OP.mult), w)
            prev_wtr = sched.event("tensor", lambda t: t.transpose(
                ptwf[:], ww[:], identf[:]), waits=[e11] + ([ev_wtev[tt - 1]] if tt >= 1 else []))
            ev_wtev[tt] = sched.event("scalar", lambda s, tt=tt: s.copy(
                wTp[:, tt * 128:(tt + 1) * 128], ptwf[:]), waits=[prev_wtr])

        def em_we(t):
            t.matmul(pwe[0][:], e2sb[:], wTp[:, 0:512], start=True, stop=True)
            return t.matmul(pwe[1][:], e2sb[:], wTp[:, 512:1024], start=True, stop=True)
        ev_wem = sched.event("tensor", em_we, waits=[consts[3]] + ev_wtev)

        def em_wee(s):
            s.copy(wexp[:, 0:512], pwe[0][:])
            return s.copy(wexp[:, 512:1024], pwe[1][:])
        ev_wee = sched.event("scalar", em_wee, waits=[ev_wem])

        def em_h(t):
            for c in range(NCHUNK):
                t.matmul(ph[0][:], abf[:, c:c + 1, :].rearrange('p a n -> p (a n)'), xn2T[:, c * 1024:c * 1024 + 512],
                         start=(c == 0), stop=(c == NCHUNK - 1))
                ins = t.matmul(ph[1][:], abf[:, c:c + 1, :].rearrange('p a n -> p (a n)'),
                               xn2T[:, c * 1024 + 512:(c + 1) * 1024],
                               start=(c == 0), stop=(c == NCHUNK - 1))
            return ins
        ev_hm = sched.event("tensor", em_h, waits=[ev_ab] + ev45)

        def em_he(s):
            s.copy(h_sb[:, 0:512], ph[0][:])
            return s.copy(h_sb[:, 512:1024], ph[1][:])
        ev_he = sched.event("scalar", em_he, waits=[ev_hm])
        ev_hw = ev5v(lambda v: v.tensor_tensor(h_wT[:], h_sb[:], wexp[:], OP.mult),
                     [ev_he, ev_wee])

        p5_mid = sched.last()
        p5ps1.close()
        p5m.close()

        p5b = contextlib.ExitStack()
        ws5 = [p5b.enter_context(nc.sbuf_tensor(f"ws5_{i}", [128, 32, 512], BF16, side="right")) for i in range(2)]
        wbstb = [p5b.enter_context(nc.sbuf_tensor(f"wbstb{i}", [128, 512], BF16, side="right")) for i in range(2)]
        x2r5 = [p5b.enter_context(nc.sbuf_tensor(f"x2r5{i}", [128, 512], F32, side="right")) for i in range(2)]
        x3st = [p5b.enter_context(nc.sbuf_tensor(f"x3st{i}", [128, 512], F32, side="right")) for i in range(2)]
        p5ps2 = contextlib.ExitStack()
        NB5 = 4
        y2ps = [p5ps2.enter_context(nc.psum_tensor(f"y2ps{i}", [128, 512], F32)) for i in range(NB5)]

        sched.set_gate(p5_mid)
        wbr = wbT.rearrange("(q p) o -> p q o", p=128)
        ev_w5 = {}
        ev_wb1 = {}
        ev_y2 = {}
        ev_y2ev = {}
        ev_x2r = {}
        ev_outw = {}
        for os_ in range(8):
            w = []
            if os_ >= 2:
                w.append(ev_y2[(os_ - 2, 7)])
            ev_w5[os_] = sched.event(
                "gpsimd",
                lambda g, os_=os_: g.dma_start(
                    ws5[os_ % 2][:], wbr[:, 0:32, os_ * 512:(os_ + 1) * 512]),
                waits=w, inc=16, sem=f"c_wb{os_ % 2}")
            w = []
            if os_ >= 2:
                w.append(ev_y2[(os_ - 2, 7)])
            ev_wb1[os_] = sched.event(
                "gpsimd", lambda g, os_=os_: g.dma_start(
                    wbstb[os_ % 2][:], wbT[4096:4224, os_ * 512:(os_ + 1) * 512]),
                waits=w, inc=16, sem=f"c_wbb{os_ % 2}")
            for tt in range(8):
                j = os_ * 8 + tt
                w = [ev_w5[os_], ev_wb1[os_]]
                if j == 0:
                    w.extend([ev_hw, ev_bb, ev_oc])
                if j >= NB5:
                    w.append(ev_y2ev[((j - NB5) // 8, (j - NB5) % 8)])

                def em_y2(t, os_=os_, tt=tt, j=j):
                    for dc in range(32):
                        t.matmul(y2ps[j % NB5][:],
                                 xn2T[:, dc * 1024 + tt * 128:dc * 1024 + (tt + 1) * 128],
                                 ws5[os_ % 2][:, dc:dc + 1, :].rearrange("p a n -> p (a n)"),
                                 start=(dc == 0), stop=False)
                    t.matmul(y2ps[j % NB5][:], onechunk[:], wbstb[os_ % 2][:],
                             start=False, stop=False)
                    return t.matmul(y2ps[j % NB5][:], h_wT[:, tt * 128:(tt + 1) * 128],
                                    bbf[:, os_ * 512:(os_ + 1) * 512],
                                    start=False, stop=True)
                ev_y2[(os_, tt)] = sched.event("tensor", em_y2, waits=w)

                w = []
                if j >= 2:
                    w.append(ev_y2ev[((j - 2) // 8, (j - 2) % 8)])
                ev_x2r[(os_, tt)] = sched.event(
                    "sync", lambda s, os_=os_, tt=tt: s.dma_start(
                        x2r5[tt % 2][:],
                        x2scr[tt * 128:(tt + 1) * 128, os_ * 512:(os_ + 1) * 512]),
                    waits=w, inc=16, sem=f"c_x2r{tt % 2}")
                w = [ev_y2[(os_, tt)], ev_x2r[(os_, tt)]]
                if j >= 2:
                    w.append(ev_outw[((j - 2) // 8, (j - 2) % 8)])
                ev_y2ev[(os_, tt)] = sched.event(
                    "vector", lambda v, os_=os_, tt=tt, j=j: v.tensor_tensor(
                        x3st[tt % 2][:], y2ps[j % NB5][:], x2r5[tt % 2][:], OP.add),
                    waits=w)
                ev_outw[(os_, tt)] = sched.event(
                    "sync", lambda s, os_=os_, tt=tt: s.dma_start(
                        out_d[tt * 128:(tt + 1) * 128, os_ * 512:(os_ + 1) * 512],
                        x3st[tt % 2][:]),
                    waits=[ev_y2ev[(os_, tt)]], inc=16, sem=f"c_out{tt % 2}")
        p5b.close()
        p5ps2.close()

        totals = sched.finalize()
        stuck = sched.check_deadlock()
        assert stuck is None, f"deadlock: {stuck}"

        semstack = contextlib.ExitStack()
        with semstack:
            sems = {}
            for name in sorted(totals.keys()):
                sems[name] = semstack.enter_context(nc.semaphore(f"sem_{name}"))
            with nc.Block() as block:

                @block.tensor
                def _(t):
                    sched.body("tensor", t, sems)

                @block.scalar
                def _(s):
                    sched.body("scalar", s, sems)

                @block.vector
                def _(v):
                    sched.body("vector", v, sems)

                @block.gpsimd
                def _(g):
                    sched.body("gpsimd", g, sems)

                @block.sync
                def _(s):
                    sched.body("sync", s, sems)
                    for name in sorted(totals.keys()):
                        s.wait_ge(sems[name], totals[name])
    return nc


_NC_CACHE = []


def kernel(L0_attn_x, L0_attn_wq, L0_attn_wk, L0_attn_wv, L0_attn_wo,
           L0_ffn_W_base, L0_ffn_b_base, L0_ffn_W_router, L0_ffn_lora_A,
           L0_ffn_lora_B):
    x = np.ascontiguousarray(np.asarray(L0_attn_x, dtype=np.float32).reshape(T, D))
    wq = np.ascontiguousarray(np.asarray(L0_attn_wq, dtype=np.float32))
    wk = np.asarray(L0_attn_wk, dtype=np.float32)
    wv = np.asarray(L0_attn_wv, dtype=np.float32)
    wo = np.ascontiguousarray(np.asarray(L0_attn_wo, dtype=np.float32))
    wb = np.asarray(L0_ffn_W_base, dtype=np.float32)
    bb = np.asarray(L0_ffn_b_base, dtype=np.float32)
    wr = np.asarray(L0_ffn_W_router, dtype=np.float32)
    la_ = np.asarray(L0_ffn_lora_A, dtype=np.float32)
    lb = np.asarray(L0_ffn_lora_B, dtype=np.float32)

    wkv = np.ascontiguousarray(np.concatenate([wk, wv], axis=1))
    wbT = np.ascontiguousarray(
        np.concatenate([wb.T, bb[None, :], np.zeros((127, D), np.float32)], axis=0))
    aflat = np.ascontiguousarray(la_.reshape(128, D).T)
    bflat = np.ascontiguousarray(lb.transpose(0, 2, 1).reshape(128, D))
    wrt = np.ascontiguousarray(
        np.concatenate([wr.T, np.zeros((D, 120), np.float32)], axis=1))
    ident = np.eye(128, dtype=bf)
    tri = np.ascontiguousarray(
        ((np.arange(512)[None, :] % 128) >= np.arange(128)[:, None]).astype(bf))
    e2 = np.zeros((128, 128), np.float32)
    for e in range(8):
        e2[e, e * 16:(e + 1) * 16] = 2.0
    e2 = np.ascontiguousarray(e2.astype(bf))

    if not _NC_CACHE:
        _NC_CACHE.append(build_nc())
    nc = _NC_CACHE[0]

    in_maps = []
    for i in range(8):
        pos2log, pads = core_layout(i)
        xkv = np.zeros((T, D), np.float32)
        for p, l in enumerate(pos2log):
            if l is not None:
                xkv[p * 128:(p + 1) * 128] = x[l * 128:(l + 1) * 128]
        biasv = np.full((128, 8), SHIFT, np.float32)
        for p in pads:
            biasv[:, p] = PADBIAS
        in_maps.append({
            "xkv": xkv, "wq": wq, "wo": wo, "wbT": wbT, "wkv": wkv,
            "aflat": aflat, "bflat": bflat, "wrt": wrt, "identc": ident,
            "tric": tri, "biasvc": np.ascontiguousarray(biasv), "e2c": e2,
        })

    res = run_bass_kernel_spmd(nc, in_maps, core_ids=list(range(8)))

    out = np.empty((T, D), np.float32)
    for i in range(8):
        oc = np.asarray(res.results[i]["out"])
        for k in range(8):
            out[(8 * k + i) * 128:(8 * k + i + 1) * 128] = oc[k * 128:(k + 1) * 128]
    return out.reshape(1, T, D)


# version 11
# speedup vs baseline: 1.1489x; 1.1489x over previous
import contextlib
import numpy as np
import ml_dtypes
import concourse.bass as bass
import concourse.mybir as mybir
from concourse.bass_utils import run_bass_kernel_spmd
from concourse.alu_op_type import AluOpType as OP

F32 = mybir.dt.float32
BF16 = mybir.dt.bfloat16
AF = mybir.ActivationFunctionType
bf = ml_dtypes.bfloat16

D = 4096
T = 8192
NPOS = 64
NOWN = 8
NSLOT = 8
NGRP = 8
GH = 4
NCHUNK = 32
SCALE = 1.0 / float(np.sqrt(128.0))
SHIFT = -10.0
NU = 10
EPS = 1e-5


class Event:
    __slots__ = ("engine", "emit", "waits", "idx", "inc", "sem")

    def __init__(self, engine, emit, waits, inc, sem):
        self.engine = engine
        self.emit = emit
        self.waits = list(waits)
        self.idx = None
        self.inc = inc
        self.sem = sem


class Sched:
    ENGINES = ("tensor", "scalar", "vector", "gpsimd", "sync")

    def __init__(self):
        self.ev = {e: [] for e in self.ENGINES}
        self.sem_counts = {}
        self.final = False
        self._gate = {e: [] for e in self.ENGINES}

    def set_gate(self, events):
        evs = [e for e in events if e is not None]
        for eng in self.ENGINES:
            self._gate[eng] = list(evs)

    def event(self, engine, emit, waits=(), inc=1, sem=None):
        w = list(waits)
        if self._gate[engine]:
            w.extend(x for x in self._gate[engine]
                     if not (x.engine == engine and x.sem == engine))
            self._gate[engine] = []
        e = Event(engine, emit, w, inc, sem or engine)
        self.ev[engine].append(e)
        return e

    def last(self):
        seen = {}
        for eng in self.ENGINES:
            for e in self.ev[eng]:
                seen[e.sem] = e
        return list(seen.values())

    def finalize(self):
        order = []
        for eng in self.ENGINES:
            order.extend(self.ev[eng])
        counts = {}
        for eng in self.ENGINES:
            for e in self.ev[eng]:
                counts[e.sem] = counts.get(e.sem, 0) + e.inc
                e.idx = counts[e.sem]
        self.sem_counts = counts
        self.final = True
        return dict(counts)

    def body(self, engine, eng_obj, sems):
        assert self.final
        for e in self.ev[engine]:
            for w in e.waits:
                assert w.idx is not None
                eng_obj.wait_ge(sems[w.sem], w.idx)
            ins = e.emit(eng_obj)
            assert ins is not None, f"emit returned None on {engine}"
            ins.then_inc(sems[e.sem], e.inc)

    def check_deadlock(self):
        pc = {e: 0 for e in self.ev}
        counts = {}
        progress = True
        while progress:
            progress = False
            for eng, lst in self.ev.items():
                while pc[eng] < len(lst):
                    e = lst[pc[eng]]
                    if all(counts.get(w.sem, 0) >= w.idx for w in e.waits):
                        counts[e.sem] = counts.get(e.sem, 0) + e.inc
                        pc[eng] += 1
                        progress = True
                    else:
                        break
        stuck = {e: (pc[e], len(self.ev[e])) for e in self.ev
                 if pc[e] < len(self.ev[e])}
        return stuck or None

    def validate_dma_sems(self):
        return None


def pcol(jj):
    return (jj % 8) * 8 + jj // 8



def batches_for(nj):
    return [(j, min(2, nj - j)) for j in range(0, nj, 2)]


def sgroups_for(nj):
    return [(j, min(6, nj - j)) for j in range(0, nj, 6)]


def add_attention(sched, alloc, palloc, kT, V, qT, rawT, mmsb, bias10,
                  ones_sq, qready=None, filler=None, first_kv_waits=()):
    sps = [palloc(f"at_sps{i}", [128, 1024], F32) for i in range(2)]
    ppv = [palloc(f"at_ppv{i}", [128, 512], F32) for i in range(2)]
    psum_s = palloc("at_psum_s", [128, 512], F32)
    uring = alloc("at_uring", [128, NU * 512], BF16)
    sacc = [alloc(f"at_sacc{i}", [128, 512], F32) for i in range(2)]
    tp = [alloc(f"at_tp{i}", [128, 512], BF16) for i in range(3)]
    sden = [alloc(f"at_sden{i}", [128, 512], F32) for i in range(2)]
    sinvb = sden

    def qslice(g, k):
        off = (k * 32 + g * GH) * 128
        return qT[:, off:off + 512]

    def u_t(gj, n=1):
        s = (gj % NU) * 512
        return uring[:, s:s + n * 512]

    st = {"exp_events": [], "pair_ev_by_gj": [], "pv_ev_by_gj": [],
          "vprev": None, "gj": 0, "gb": 0}
    fin = {}

    def ev_t(emit, waits=()):
        return sched.event("tensor", emit, waits)

    def ev_a(emit, waits=()):
        return sched.event("scalar", emit, waits)

    def ev_v(emit, waits=()):
        w = list(waits)
        if st["vprev"] is not None:
            w.append(st["vprev"])
        e = sched.event("vector", emit, w)
        st["vprev"] = e
        return e

    def make_ssum(x, f):
        waits = [f["sg_last"]]
        if x - 1 in fin and "sevac" in fin[x - 1]:
            waits.append(fin[x - 1]["sevac"])
        f["ssum"] = ev_t(lambda t, x=x: t.matmul(
            psum_s[:], ones_sq[:], sacc[x % 2][:], start=True, stop=True), waits)

    def make_sevac(x, f):
        waits = [f["ssum"]]
        if x - 2 in fin and "raw" in fin[x - 2]:
            waits.append(fin[x - 2]["raw"])
        f["sevac"] = ev_a(lambda s, x=x: s.copy(sden[x % 2][:], psum_s[:]), waits)

    def make_recip(x, f):
        waits = [f["sevac"]]
        if x - 2 in fin and "raw" in fin[x - 2]:
            waits.append(fin[x - 2]["raw"])
        f["recip"] = ev_v(lambda v, x=x: v.reciprocal(
            sinvb[x % 2][:], sden[x % 2][:]), waits)

    def make_raw(x, f, g, k):
        waits = [f["recip"], f["p_last"]]
        off = (k * 32 + g * GH) * 128
        f["raw"] = ev_v(lambda v, off=off, x=x: v.tensor_tensor(
            rawT[:, off:off + 512], ppv[x % 2][:], sinvb[x % 2][:], OP.mult),
            waits)

    entries = [(g, k) for g in range(NGRP) for k in range(NSLOT)]
    for ei, (g, k) in enumerate(entries):
        if filler is not None:
            filler()
        nj = 8 * k + 8
        bl = batches_for(nj)
        nb = len(bl)
        sg = sgroups_for(nj)
        gj0 = st["gj"]
        gb0 = st["gb"]
        f = {}
        fin[ei] = f
        fprev = fin.get(ei - 1)
        exp_ev = [None] * nb
        s_ev = [None] * nb
        p_ev = [None] * nb
        me = [None] * 8
        glast = {}
        for gi, (j0, glen) in enumerate(sg):
            glast.setdefault((j0 + glen) // 2 - 1, []).append(gi)
        first_acc = [True]

        def scores_event(bi):
            j0, b = bl[bi]
            gbi = gb0 + bi
            waits = []
            if gbi >= 2:
                waits.append(st["exp_events"][gbi - 2])
            if bi == 0 and k == 0 and qready is not None:
                waits.append(qready(g))
            if ei == 0 and bi == 0:
                waits.extend(first_kv_waits)

            def emit(t, j0=j0, b=b, bi=bi, g=g, k=k):
                for jj in range(j0, j0 + b):
                    c = pcol(jj)
                    ins = t.matmul(
                        sps[bi % 2][:, (jj - j0) * 512:(jj - j0 + 1) * 512],
                        kT[:, c * 128:(c + 1) * 128],
                        qslice(g, k), start=True, stop=True)
                return ins
            s_ev[bi] = ev_t(emit, waits)

        def exp_event(bi):
            j0, b = bl[bi]
            gj = gj0 + j0
            waits = [s_ev[bi]]
            if gj + b - 1 >= NU:
                waits.append(st["pair_ev_by_gj"][gj + b - 1 - NU])
                waits.append(st["pv_ev_by_gj"][gj + b - 1 - NU])

            def emit(s, j0=j0, b=b, bi=bi, gj=gj):
                return s.activation(u_t(gj, b), sps[bi % 2][:, 0:b * 512],
                                    AF.Exp, bias=bias10[:], scale=SCALE)
            e = ev_a(emit, waits)
            exp_ev[bi] = e
            st["exp_events"].append(e)
            for jj in range(j0, j0 + b):
                if jj >= nj - 8:
                    bb = jj - (nj - 8)
                    me[bb] = sched.event(
                        "gpsimd", lambda g_, jj=jj, bb=bb, gj0=gj0:
                        g_.tensor_tensor(u_t(gj0 + jj), u_t(gj0 + jj),
                                         mmsb[:, bb * 512:(bb + 1) * 512],
                                         OP.mult), [e])

        def pv_event(bi):
            j0, b = bl[bi]
            waits = [exp_ev[bi]]
            if bi == 0 and ei >= 2:
                waits.append(fin[ei - 2]["raw"])
            for jj in range(j0, j0 + b):
                if jj >= nj - 8:
                    waits.append(me[jj - (nj - 8)])

            def emit(t, j0=j0, b=b, ei=ei, nj=nj, gj0=gj0):
                for jj in range(j0, j0 + b):
                    c = pcol(jj)
                    ins = t.matmul(ppv[ei % 2][:], V[:, c * 128:(c + 1) * 128],
                                   u_t(gj0 + jj), start=(jj == 0),
                                   stop=(jj == nj - 1))
                return ins
            p_ev[bi] = ev_t(emit, waits)
            for jj in range(j0, j0 + b):
                st["pv_ev_by_gj"].append(p_ev[bi])

        def tree_group(gi):
            j0, glen = sg[gi]
            npair = glen // 2
            pair_evs = []
            for pi in range(npair):
                jj = j0 + 2 * pi
                w = [exp_ev[jj // 2]]
                if jj >= nj - 8:
                    w.append(me[jj - (nj - 8)])
                    w.append(me[jj + 1 - (nj - 8)])
                e = ev_v(lambda v, pi=pi, jj=jj, gj0=gj0: v.tensor_tensor(
                    tp[pi][:], u_t(gj0 + jj), u_t(gj0 + jj + 1), OP.add), w)
                pair_evs.append(e)
                st["pair_ev_by_gj"].append(e)
                st["pair_ev_by_gj"].append(e)
            start_pi = 0
            if first_acc[0]:
                w = []
                if ei - 2 in fin and "ssum" in fin[ei - 2]:
                    w.append(fin[ei - 2]["ssum"])
                if npair >= 2:
                    ev_v(lambda v, ei=ei: v.tensor_tensor(
                        sacc[ei % 2][:], tp[0][:], tp[1][:], OP.add), w)
                    start_pi = 2
                else:
                    ev_v(lambda v, ei=ei: v.tensor_scalar(
                        sacc[ei % 2][:], tp[0][:], 0.0, None, OP.add), w)
                    start_pi = 1
                first_acc[0] = False
            for pi in range(start_pi, npair):
                ev_v(lambda v, pi=pi, ei=ei: v.tensor_tensor(
                    sacc[ei % 2][:], sacc[ei % 2][:], tp[pi][:], OP.add), [])
            f["sg_last"] = st["vprev"]

        for b in range(nb):
            scores_event(b)
            exp_event(b)
            if fprev:
                if b == 0:
                    make_ssum(ei - 1, fprev)
                    make_sevac(ei - 1, fprev)
                elif b == 1:
                    make_recip(ei - 1, fprev)
                elif b == 2:
                    make_raw(ei - 1, fprev, *entries[ei - 1])
            if b >= 2:
                pv_event(b - 2)
            for gi in glast.get(b, []):
                tree_group(gi)
        pv_event(nb - 2)
        pv_event(nb - 1)
        f["p_last"] = p_ev[nb - 1]
        assert len(st["pv_ev_by_gj"]) == gj0 + nj
        assert len(st["pair_ev_by_gj"]) == gj0 + nj
        st["gj"] += nj
        st["gb"] += nb

    last = len(entries) - 1
    f = fin[last]
    g, k = entries[last]
    make_ssum(last, f)
    make_sevac(last, f)
    make_recip(last, f)
    make_raw(last, f, g, k)
    return fin


def build_nc():
    nc = bass.Bass("TRN2", target_bir_lowering=False)
    xown = nc.dram_tensor("xown", [1024, D], F32, kind="ExternalInput").ap()
    wq = nc.dram_tensor("wq", [D, D], F32, kind="ExternalInput").ap()
    wo = nc.dram_tensor("wo", [D, D], F32, kind="ExternalInput").ap()
    wbT = nc.dram_tensor("wbT", [D + 128, D], F32, kind="ExternalInput").ap()
    wkv = nc.dram_tensor("wkv", [D, 256], F32, kind="ExternalInput").ap()
    aflat = nc.dram_tensor("aflat", [D, 128], F32, kind="ExternalInput").ap()
    bflat = nc.dram_tensor("bflat", [128, D], F32, kind="ExternalInput").ap()
    wrt = nc.dram_tensor("wrt", [D, 128], F32, kind="ExternalInput").ap()
    ident_in = nc.dram_tensor("identc", [128, 128], BF16, kind="ExternalInput").ap()
    mmask_in = nc.dram_tensor("mmaskc", [128, 4096], BF16, kind="ExternalInput").ap()
    e2_in = nc.dram_tensor("e2c", [128, 128], BF16, kind="ExternalInput").ap()
    out_d = nc.dram_tensor("out", [1024, D], F32, kind="ExternalOutput").ap()
    x2scr = nc.dram_tensor("x2scr", [1024, D], F32).ap()
    kvout = nc.dram_tensor("kvout", [128, 2048], BF16)
    kvg = nc.dram_tensor("kvg", [1024, 2048], BF16)

    sched = Sched()
    left = contextlib.ExitStack()

    def la(name, shape, dtype):
        return left.enter_context(nc.sbuf_tensor(name, shape, dtype, side="left"))

    with left:
        ident = la("identsb", [128, 128], BF16)
        mmscope = contextlib.ExitStack()
        mmsb = mmscope.enter_context(
            nc.sbuf_tensor("mmsb", [128, 4096], BF16, side="right"))
        bias10 = la("bias10", [128, 1], F32)
        eps_t = la("epst", [128, 1], F32)
        ones_sq = la("ones_sq", [128, 128], F32)
        xnT = la("xnT", [128, NCHUNK * 1024], BF16)
        kvscope = contextlib.ExitStack()
        kT = kvscope.enter_context(nc.sbuf_tensor("kTs", [128, NPOS * 128], BF16, side="left"))
        V = kvscope.enter_context(nc.sbuf_tensor("Vs", [128, NPOS * 128], BF16, side="left"))

        sched.event("sync", lambda s: s.dma_start(ident[:], ident_in), inc=16, sem="c_setup")
        ev_setup = sched.event("sync", lambda s: s.dma_start(mmsb[:], mmask_in), inc=16, sem="c_setup")
        ev_mb = sched.event("gpsimd", lambda g: g.memset(bias10[:], SHIFT))
        ev_me = sched.event("gpsimd", lambda g: g.memset(eps_t[:], EPS))
        ev_mo = sched.event("gpsimd", lambda g: g.memset(ones_sq[:], 1.0))
        consts = [ev_setup, ev_mb, ev_me, ev_mo]
        sched.set_gate(consts)

        p1 = contextlib.ExitStack()

        def r1(name, shape, dtype):
            return p1.enter_context(nc.sbuf_tensor(name, shape, dtype, side="right"))

        xst = [r1(f"xst{i}", [128, D], F32) for i in range(3)]
        xn_bf = [r1(f"xn_bf{i}", [128, D], BF16) for i in range(2)]
        wkvbf = r1("wkvbf", [128, NCHUNK, 256], BF16)
        kTown = r1("kTown", [128, 1024], BF16)
        Vown = r1("Vown", [128, 1024], BF16)
        stats = r1("stats", [128, 48], F32)
        mv = [r1(f"mv{i}", [128, 2], F32) for i in range(4)]
        rstd = [r1(f"rstd{i}", [128, 1], F32) for i in range(4)]
        negmur = [r1(f"negmur{i}", [128, 1], F32) for i in range(4)]
        lnv = [r1(f"lnv{i}", [128, 1], F32) for i in range(4)]

        p1ps = contextlib.ExitStack()
        ptx = p1ps.enter_context(nc.psum_tensor("ptx", [128, D], BF16))
        kps = [p1ps.enter_context(nc.psum_tensor(f"kps{i}", [128, 256], F32)) for i in range(2)]
        vps = [p1ps.enter_context(nc.psum_tensor(f"vps{i}", [128, 128], F32)) for i in range(2)]

        ev_wkv = sched.event(
            "gpsimd",
            lambda g: g.dma_start(wkvbf[:], wkv.rearrange("(c p) h -> p c h", p=128)),
            waits=consts, inc=16, sem="c_wkv")

        ev_x = [None] * NOWN
        ev_apply = [None] * NOWN
        ev_applyA = [None] * NOWN
        ev_tx = [None] * NOWN
        ev_txevac = [None] * NOWN
        ev_kv = [None] * (NOWN // 2)
        ev_kvevac = [None] * (NOWN // 2)
        ev_vmm = [None] * NOWN
        ev_vev = [None] * NOWN
        ev_ag_prev = [None]
        ev_nm_l = [None] * NOWN

        def stageA(p):
            w = [ev_setup] if p < 3 else [ev_apply[p - 3], ev_applyA[p - 3]]
            ev_x[p] = sched.event(
                "sync",
                lambda s, p=p: s.dma_start(xst[p % 3][:], xown[p * 128:(p + 1) * 128, :]),
                waits=w, inc=16, sem=f"c_x{p % 3}")

            def em_stats(v, p=p):
                for c in range(8):
                    ins = v.bn_stats(stats[:, c * 6:(c + 1) * 6],
                                     xst[p % 3][:, c * 512:(c + 1) * 512])
                return ins
            w = [ev_x[p]]
            if ev_ag_prev[0] is not None:
                w.append(ev_ag_prev[0])
            ev_st = sched.event("vector", em_stats, waits=w)
            ev_ag = sched.event("vector",
                                lambda v, p=p: v.bn_aggr(mv[p % 4][:], stats[:]),
                                waits=[ev_st])
            ev_ag_prev[0] = ev_ag
            wln = [ev_ag]
            if p >= 4:
                wln.append(ev_apply[p - 4])
            ev_ln = sched.event("scalar", lambda s, p=p: s.activation(
                lnv[p % 4][:], mv[p % 4][:, 1:2], AF.Ln, bias=eps_t[:], scale=1.0),
                waits=wln)
            ev_rs = sched.event("scalar", lambda s, p=p: s.activation(
                rstd[p % 4][:], lnv[p % 4][:], AF.Exp, bias=0.0, scale=-0.5),
                waits=[ev_ln])
            ev_nm_l[p] = sched.event("vector", lambda v, p=p: v.tensor_scalar(
                negmur[p % 4][:], mv[p % 4][:, 0:1], rstd[p % 4][:], -1.0,
                OP.mult, OP.mult), waits=[ev_rs])

        def stageB(p):
            wap = [ev_nm_l[p]]
            if p >= 2:
                wap.append(ev_tx[p - 2])
            ev_apA = sched.event("scalar", lambda s, p=p: s.activation(
                xn_bf[p % 2][:, 0:2048], xst[p % 3][:, 0:2048], AF.Identity,
                bias=negmur[p % 4][:], scale=rstd[p % 4][:]), waits=wap)
            ev_apB = sched.event("vector", lambda v, p=p: v.tensor_scalar(
                xn_bf[p % 2][:, 2048:4096], xst[p % 3][:, 2048:4096],
                rstd[p % 4][:], negmur[p % 4][:], OP.mult, OP.add), waits=wap)
            ev_apply[p] = ev_apB
            ev_applyA[p] = ev_apA
            wtx = [ev_apA, ev_apB]
            if p >= 1:
                wtx.append(ev_txevac[p - 1][0])
                wtx.append(ev_txevac[p - 1][1])

            def em_tx(t, p=p):
                for c in range(NCHUNK):
                    ins = t.transpose(ptx[:, c * 128:(c + 1) * 128],
                                      xn_bf[p % 2][:, c * 128:(c + 1) * 128], ident[:])
                return ins
            ev_tx[p] = sched.event("tensor", em_tx, waits=wtx)

            wtev = [ev_tx[p]]
            if p // 2 >= 1:
                wtev.append(ev_kv[p // 2 - 1])
            if p >= 2:
                wtev.append(ev_vmm[p - 2])

            def em_teA(s, p=p):
                return s.copy(
                    xnT[:].rearrange("q (c n) -> q c n", c=NCHUNK)[:, 0:16, p * 128:(p + 1) * 128],
                    ptx[:].rearrange("q (c n) -> q c n", c=NCHUNK)[:, 0:16, :])

            def em_teB(v, p=p):
                return v.tensor_copy(
                    xnT[:].rearrange("q (c n) -> q c n", c=NCHUNK)[:, 16:32, p * 128:(p + 1) * 128],
                    ptx[:].rearrange("q (c n) -> q c n", c=NCHUNK)[:, 16:32, :])
            ev_teA = sched.event("scalar", em_teA, waits=wtev)
            ev_teB = sched.event("vector", em_teB, waits=wtev)
            ev_txevac[p] = (ev_teA, ev_teB)

        def stageB2(p):
            q4, r4 = p // 2, p % 2
            ev_teA, ev_teB = ev_txevac[p]
            wv_ = [ev_teA, ev_teB, ev_wkv]
            if p >= 2:
                wv_.append(ev_vev[p - 2])

            def em_v(t, p=p):
                for c in range(NCHUNK):
                    ins = t.matmul(vps[p % 2][:],
                                   xnT[:, c * 1024 + p * 128:c * 1024 + (p + 1) * 128],
                                   wkvbf[:, c:c + 1, 128:256].rearrange('p a n -> p (a n)'),
                                   start=(c == 0), stop=(c == NCHUNK - 1))
                return ins
            ev_vmm[p] = sched.event("tensor", em_v, waits=wv_)

            if r4 == 1:
                wkvq = [e for j in range(2) for e in ev_txevac[2 * q4 + j]] + [ev_wkv]
                if q4 >= 2:
                    wkvq.append(ev_kvevac[q4 - 2])

                def em_kv(t, q4=q4):
                    for c in range(NCHUNK):
                        ins = t.matmul(kps[q4 % 2][:], wkvbf[:, c:c + 1, 0:128].rearrange('p a n -> p (a n)'),
                                       xnT[:, c * 1024 + q4 * 256:c * 1024 + (q4 + 1) * 256],
                                       start=(c == 0), stop=(c == NCHUNK - 1))
                    return ins
                ev_kv[q4] = sched.event("tensor", em_kv, waits=wkvq)

        def stageB3(p):
            q4, r4 = p // 2, p % 2
            ev_vev[p] = sched.event(
                "scalar", lambda s, p=p: s.copy(Vown[:, p * 128:(p + 1) * 128], vps[p % 2][:]),
                waits=[ev_vmm[p]])
            if r4 == 1:
                def em_kvev(s, q4=q4):
                    return s.copy(kTown[:, q4 * 256:(q4 + 1) * 256], kps[q4 % 2][:])
                ev_kvevac[q4] = sched.event("scalar", em_kvev, waits=[ev_kv[q4]])

        for p in range(NOWN):
            stageA(p)
            if p >= 4:
                stageB3(p - 4)
            if p >= 2:
                stageB2(p - 2)
            if p >= 1:
                stageB(p - 1)
        stageB3(NOWN - 4)
        stageB2(NOWN - 2)
        stageB3(NOWN - 3)
        stageB(NOWN - 1)
        stageB2(NOWN - 1)
        stageB3(NOWN - 2)
        stageB3(NOWN - 1)

        p1_tail = sched.last()

        ev_ko = sched.event(
            "sync", lambda s: s.dma_start(kvout.ap()[:, 0:1024], kTown[:]),
            waits=[ev_kvevac[i] for i in range(NOWN // 2)], inc=16, sem="c_kvo")
        ev_vo = sched.event(
            "sync", lambda s: s.dma_start(kvout.ap()[:, 1024:2048], Vown[:]),
            waits=[ev_vev[i] for i in range(NOWN)], inc=16, sem="c_kvo")
        ev_cc = sched.event(
            "gpsimd", lambda g: g.collective_compute(
                "AllGather", mybir.AluOpType.bypass,
                replica_groups=[list(range(8))],
                ins=[kvout.ap().opt()], outs=[kvg.ap().opt()]),
            waits=[ev_ko, ev_vo], inc=1, sem="c_cc")
        ev_kld = sched.event(
            "sync", lambda s: s.dma_start(
                kT[:].rearrange("p (r n) -> p r n", r=8),
                kvg.ap().rearrange("(r d) n -> d r n", d=128)[:, :, 0:1024]),
            waits=[ev_cc], inc=16, sem="c_kld")
        ev_vld = sched.event(
            "sync", lambda s: s.dma_start(
                V[:].rearrange("p (r n) -> p r n", r=8),
                kvg.ap().rearrange("(r t) n -> t r n", t=128)[:, :, 1024:2048]),
            waits=[ev_cc], inc=16, sem="c_vld")
        p1.close()
        p1ps.close()
        qscope = contextlib.ExitStack()
        qT = qscope.enter_context(
            nc.sbuf_tensor("qT", [128, NSLOT * 32 * 128], BF16, side="right"))

        p2 = contextlib.ExitStack()
        wqbf = [p2.enter_context(nc.sbuf_tensor(f"wqbf{i}", [128, NCHUNK, 128], BF16, side="right")) for i in range(2)]
        p2ps = contextlib.ExitStack()
        qps = p2ps.enter_context(nc.psum_tensor("qps", [128, 512], F32))

        sched.set_gate(p1_tail)
        ev_wq = [None] * 32
        ev_qmm = [None] * 32
        ev_qev = [None] * 32
        _qstate = {"half": 0, "prev_evac": None}

        def q_emit_half():
            hid = _qstate["half"]
            if hid >= 64:
                return
            _qstate["half"] = hid + 1
            ot, hh = hid // 2, hid % 2
            if hh == 0:
                w = [] if ot < 2 else [ev_qmm[ot - 2]]
                ev_wq[ot] = sched.event(
                    "gpsimd",
                    lambda g, ot=ot: g.dma_start(
                        wqbf[ot % 2][:],
                        wq.rearrange("(c p) o -> p c o", p=128)[:, :, ot * 128:(ot + 1) * 128]),
                    waits=w, inc=16, sem=f"c_wq{ot % 2}")

            def em_q(t, ot=ot, hh=hh):
                for c in range(NCHUNK):
                    ins = t.matmul(qps[:], wqbf[ot % 2][:, c:c + 1, :].rearrange('p a n -> p (a n)'),
                                   xnT[:, c * 1024 + hh * 512:c * 1024 + (hh + 1) * 512],
                                   start=(c == 0), stop=(c == NCHUNK - 1))
                return ins
            w = [ev_wq[ot]]
            if _qstate["prev_evac"] is not None:
                w.append(_qstate["prev_evac"])
            ev_m = sched.event("tensor", em_q, waits=w)

            def em_qev(s, ot=ot, hh=hh):
                q3 = qT[:].rearrange("p (k n) -> p k n", k=8)
                return s.copy(q3[:, hh * 4:(hh + 1) * 4, ot * 128:(ot + 1) * 128],
                              qps[:].rearrange("p (k n) -> p k n", k=4))
            ev_e = sched.event("scalar", em_qev, waits=[ev_m])
            _qstate["prev_evac"] = ev_e
            if hh == 1:
                ev_qmm[ot] = ev_m
                ev_qev[ot] = ev_e

        for _ in range(16):
            q_emit_half()
        p3 = contextlib.ExitStack()
        p3ps = contextlib.ExitStack()
        def qready(g):
            e = ev_qev[4 * g + 3]
            assert e is not None, f"q outtile {4 * g + 3} not yet emitted"
            return e
        add_attention(
            sched,
            lambda n, s, d: p3.enter_context(nc.sbuf_tensor(n, s, d, side="right")),
            lambda n, s, d: p3ps.enter_context(nc.psum_tensor(n, s, d)),
            kT, V, qT, qT, mmsb, bias10, ones_sq, qready=qready,
            filler=q_emit_half, first_kv_waits=(ev_kld, ev_vld))
        p3_tail = sched.last()
        p3.close()
        p3ps.close()
        p2.close()
        p2ps.close()
        kvscope.close()

        p4 = contextlib.ExitStack()
        ws4 = [xnT[:, i * 16384:(i + 1) * 16384].rearrange("p (q o) -> p q o", q=32)
               for i in range(2)]
        xsst = [p4.enter_context(nc.sbuf_tensor(f"xsst{i}", [128, 512], F32, side="right")) for i in range(2)]
        x2st = [p4.enter_context(nc.sbuf_tensor(f"x2st{i}", [128, 512], F32, side="right")) for i in range(2)]
        p4ps = contextlib.ExitStack()
        NB4 = 4
        yps = [p4ps.enter_context(nc.psum_tensor(f"yps{i}", [128, 512], F32)) for i in range(NB4)]

        sched.set_gate(p3_tail)
        rawT4 = qT[:].rearrange("p (k n) -> p k n", k=8)
        wor = wo.rearrange("(q p) o -> p q o", p=128)
        ev_w4 = {}
        ev_ymm = {}
        ev_yev = {}
        ev_xs = {}
        ev_x2w = {}
        p3t_tensor = [e for e in p3_tail if e.sem == "tensor"]
        for os_ in range(8):
            w = []
            if os_ >= 2:
                w.append(ev_ymm[(os_ - 2, 7)])
            else:
                w.extend(p3t_tensor)
            ev_w4[os_] = sched.event(
                "gpsimd",
                lambda g, os_=os_: g.dma_start(
                    ws4[os_ % 2], wor[:, :, os_ * 512:(os_ + 1) * 512]),
                waits=w, inc=16, sem=f"c_wo{os_ % 2}")
            for tt in range(8):
                j = os_ * 8 + tt
                w = [ev_w4[os_]]
                if j >= NB4:
                    w.append(ev_yev[((j - NB4) // 8, (j - NB4) % 8)])

                def em_y(t, os_=os_, tt=tt, j=j):
                    for ic in range(32):
                        ins = t.matmul(
                            yps[j % NB4][:],
                            rawT4[:, tt, ic * 128:(ic + 1) * 128],
                            xnT[:, (os_ % 2) * 16384 + ic * 512:(os_ % 2) * 16384 + (ic + 1) * 512],
                            start=(ic == 0), stop=(ic == 31))
                    return ins
                ev_ymm[(os_, tt)] = sched.event("tensor", em_y, waits=w)
                w = []
                if j >= 2:
                    w.append(ev_yev[((j - 2) // 8, (j - 2) % 8)])
                ev_xs[(os_, tt)] = sched.event(
                    "sync", lambda s, os_=os_, tt=tt: s.dma_start(
                        xsst[tt % 2][:],
                        xown[tt * 128:(tt + 1) * 128, os_ * 512:(os_ + 1) * 512]),
                    waits=w, inc=16, sem=f"c_xs{tt % 2}")
                w = [ev_ymm[(os_, tt)], ev_xs[(os_, tt)]]
                if j >= 2:
                    w.append(ev_x2w[((j - 2) // 8, (j - 2) % 8)])
                ev_yev[(os_, tt)] = sched.event(
                    "vector", lambda v, os_=os_, tt=tt, j=j: v.tensor_tensor(
                        x2st[tt % 2][:], yps[j % NB4][:], xsst[tt % 2][:], OP.add),
                    waits=w)
                ev_x2w[(os_, tt)] = sched.event(
                    "sync", lambda s, os_=os_, tt=tt: s.dma_start(
                        x2scr[tt * 128:(tt + 1) * 128, os_ * 512:(os_ + 1) * 512],
                        x2st[tt % 2][:]),
                    waits=[ev_yev[(os_, tt)]], inc=16, sem=f"c_x2w{tt % 2}")
        p4_tail = sched.last()
        p4.close()
        p4ps.close()
        qscope.close()
        mmscope.close()
        xn2T = xnT

        p45 = contextlib.ExitStack()

        def r45(name, shape, dtype):
            return p45.enter_context(nc.sbuf_tensor(name, shape, dtype, side="right"))
        x2rd = [r45(f"x2rd{i}", [128, D], F32) for i in range(2)]
        xn2bf = [r45(f"xn2bf{i}", [128, D], BF16) for i in range(2)]
        st2 = r45("st2", [128, 48], F32)
        mv2 = [r45(f"mv2{i}", [128, 2], F32) for i in range(2)]
        rstd2 = [r45(f"rstd2{i}", [128, 1], F32) for i in range(2)]
        negmur2 = [r45(f"negmur2{i}", [128, 1], F32) for i in range(2)]
        lnv2 = [r45(f"lnv2{i}", [128, 1], F32) for i in range(2)]
        p45ps = contextlib.ExitStack()
        ptx2 = [p45ps.enter_context(nc.psum_tensor(f"ptx2{i}", [128, D], BF16)) for i in range(2)]

        sched.set_gate(p4_tail)
        ev45 = []
        apl = {}
        txl = {}
        evacl = {}
        agp = None
        for tt in range(8):
            w = [] if tt < 2 else [apl[tt - 2]]
            ev_r = sched.event("sync", lambda s, tt=tt: s.dma_start(
                x2rd[tt % 2][:], x2scr[tt * 128:(tt + 1) * 128, :]), waits=w,
                inc=16, sem=f"c_x2rd{tt % 2}")

            def em_st2(v, tt=tt):
                for c in range(8):
                    ins = v.bn_stats(st2[:, c * 6:(c + 1) * 6],
                                     x2rd[tt % 2][:, c * 512:(c + 1) * 512])
                return ins
            ev_s = sched.event("vector", em_st2, waits=[ev_r] + ([agp] if agp else []))
            ev_a2 = sched.event("vector", lambda v, tt=tt: v.bn_aggr(mv2[tt % 2][:], st2[:]),
                                waits=[ev_s])
            agp = ev_a2
            wln = [ev_a2] + ([apl[tt - 2]] if tt >= 2 else [])
            ev_l = sched.event("scalar", lambda s, tt=tt: s.activation(
                lnv2[tt % 2][:], mv2[tt % 2][:, 1:2], AF.Ln, bias=eps_t[:], scale=1.0),
                waits=wln)
            ev_rs2 = sched.event("scalar", lambda s, tt=tt: s.activation(
                rstd2[tt % 2][:], lnv2[tt % 2][:], AF.Exp, bias=0.0, scale=-0.5),
                waits=[ev_l])
            ev_n = sched.event("vector", lambda v, tt=tt: v.tensor_scalar(
                negmur2[tt % 2][:], mv2[tt % 2][:, 0:1], rstd2[tt % 2][:], -1.0,
                OP.mult, OP.mult), waits=[ev_rs2])
            wap = [ev_n] + ([txl[tt - 2]] if tt >= 2 else [])
            ev_ap = sched.event("scalar", lambda s, tt=tt: s.activation(
                xn2bf[tt % 2][:], x2rd[tt % 2][:], AF.Identity,
                bias=negmur2[tt % 2][:], scale=rstd2[tt % 2][:]), waits=wap)
            apl[tt] = ev_ap

            def em_tx2(t, tt=tt):
                for c in range(NCHUNK):
                    ins = t.transpose(ptx2[tt % 2][:, c * 128:(c + 1) * 128],
                                      xn2bf[tt % 2][:, c * 128:(c + 1) * 128], ident[:])
                return ins
            wtx = [ev_ap] + ([evacl[tt - 2][0], evacl[tt - 2][1]] if tt >= 2 else [])
            ev_t45 = sched.event("tensor", em_tx2, waits=wtx)
            txl[tt] = ev_t45

            def em_e2A(s, tt=tt):
                return s.copy(
                    xn2T[:].rearrange("q (c n) -> q c n", c=NCHUNK)[:, 0:16, tt * 128:(tt + 1) * 128],
                    ptx2[tt % 2][:].rearrange("q (c n) -> q c n", c=NCHUNK)[:, 0:16, :])

            def em_e2B(v, tt=tt):
                return v.tensor_copy(
                    xn2T[:].rearrange("q (c n) -> q c n", c=NCHUNK)[:, 16:32, tt * 128:(tt + 1) * 128],
                    ptx2[tt % 2][:].rearrange("q (c n) -> q c n", c=NCHUNK)[:, 16:32, :])
            wal = [ev_t45] + ([ev_ymm[(7, 7)]] if tt == 0 else [])
            eA = sched.event("scalar", em_e2A, waits=wal)
            eB = sched.event("vector", em_e2B, waits=wal)
            evacl[tt] = (eA, eB)
            ev45.extend([eA, eB])
        p45_tail = sched.last()
        p45.close()
        p45ps.close()

        p5 = contextlib.ExitStack()
        p5m = contextlib.ExitStack()

        def a5(name, shape, dtype):
            return p5.enter_context(nc.sbuf_tensor(name, shape, dtype, side="right"))

        def a5m(name, shape, dtype):
            return p5m.enter_context(nc.sbuf_tensor(name, shape, dtype, side="right"))
        bbf = a5("bbf", [128, D], BF16)
        h_wT = a5("h_wT", [128, 1024], BF16)
        onechunk = a5("onechunk", [128, 128], BF16)
        identf = a5m("identf", [128, 128], F32)
        e2sb = a5m("e2sb", [128, 128], BF16)
        abf = a5m("abf", [128, NCHUNK, 128], BF16)
        wrbf = a5m("wrbf", [128, NCHUNK, 128], BF16)
        lgT = a5m("lgT", [128, 1024], F32)
        lg = a5m("lg", [128, 8], F32)
        m1 = a5m("m1", [128, 1], F32)
        negm1 = a5m("negm1", [128, 1], F32)
        ee = a5m("ee", [128, 8], F32)
        mask1 = a5m("mask1", [128, 8], F32)
        lg2 = a5m("lg2", [128, 8], F32)
        m2 = a5m("m2", [128, 1], F32)
        mask2 = a5m("mask2", [128, 8], F32)
        sel = a5m("sel", [128, 8], F32)
        es = a5m("es", [128, 8], F32)
        ssum5 = a5m("ssum5", [128, 1], F32)
        srec = a5m("srec", [128, 1], F32)
        ww = a5m("ww", [128, 128], F32)
        wTp = a5m("wTp", [128, 1024], BF16)
        wexp = a5m("wexp", [128, 1024], BF16)
        h_sb = a5m("h_sb", [128, 1024], BF16)

        p5ps1 = contextlib.ExitStack()
        plg = [p5ps1.enter_context(nc.psum_tensor(f"plg{i}", [128, 512], F32)) for i in range(2)]
        ptr5 = p5ps1.enter_context(nc.psum_tensor("ptr5", [128, 128], F32))
        ptwf = p5ps1.enter_context(nc.psum_tensor("ptwf", [128, 128], F32))
        pwe = [p5ps1.enter_context(nc.psum_tensor(f"pwe{i}", [128, 512], F32)) for i in range(2)]
        ph = [p5ps1.enter_context(nc.psum_tensor(f"ph{i}", [128, 512], F32)) for i in range(2)]

        sched.set_gate(p45_tail)
        ev_ab = sched.event("gpsimd", lambda g: g.dma_start(
            abf[:], aflat.rearrange("(c p) e -> p c e", p=128)), inc=16, sem="c_la")
        ev_bb = sched.event("gpsimd", lambda g: g.dma_start(bbf[:], bflat),
                            inc=16, sem="c_lb")
        ev_wr = sched.event("gpsimd", lambda g: g.dma_start(
            wrbf[:], wrt.rearrange("(c p) e -> p c e", p=128)), inc=16, sem="c_lr")
        ev_e2 = sched.event("gpsimd", lambda g: g.dma_start(e2sb[:], e2_in),
                            inc=16, sem="c_le")
        ev_wwz = sched.event("gpsimd", lambda g: g.memset(ww[:], 0.0))
        ev_idf = sched.event("scalar", lambda s: s.copy(identf[:], ident[:]))
        ev_oc0 = sched.event("gpsimd", lambda g: g.memset(onechunk[:], 0.0))
        ev_oc = sched.event("gpsimd", lambda g: g.memset(onechunk[0:1, :], 1.0),
                            waits=[ev_oc0])

        def em_lg(t):
            for c in range(NCHUNK):
                t.matmul(plg[0][:], wrbf[:, c:c + 1, :].rearrange('p a n -> p (a n)'), xn2T[:, c * 1024:c * 1024 + 512],
                         start=(c == 0), stop=(c == NCHUNK - 1))
                ins = t.matmul(plg[1][:], wrbf[:, c:c + 1, :].rearrange('p a n -> p (a n)'),
                               xn2T[:, c * 1024 + 512:(c + 1) * 1024],
                               start=(c == 0), stop=(c == NCHUNK - 1))
            return ins
        ev_lgm = sched.event("tensor", em_lg, waits=[ev_wr] + ev45)

        def em_lge(s):
            s.copy(lgT[:, 0:512], plg[0][:])
            return s.copy(lgT[:, 512:1024], plg[1][:])
        ev_lge = sched.event("scalar", em_lge, waits=[ev_lgm])

        vch = [None]

        def ev5v(emit, waits=()):
            w = list(waits)
            if vch[0] is not None:
                w.append(vch[0])
            e = sched.event("vector", emit, w)
            vch[0] = e
            return e

        ev_wtev = [None] * 8
        prev_lgx = None
        prev_wtr = None
        for tt in range(8):
            w = [ev_lge]
            if tt == 0:
                w.append(ev_idf)
            if prev_lgx is not None:
                w.append(prev_lgx)
            ev_ltr = sched.event("tensor", lambda t, tt=tt: t.transpose(
                ptr5[:], lgT[:, tt * 128:(tt + 1) * 128], identf[:]), waits=w)
            prev_lgx = sched.event("scalar", lambda s: s.copy(lg[:], ptr5[:, 0:8]),
                                   waits=[ev_ltr])
            ev5v(lambda v: v.reduce_max(m1[:], lg[:], axis=mybir.AxisListType.X), [prev_lgx])
            ev5v(lambda v: v.tensor_scalar(negm1[:], m1[:], -1.0, None, OP.mult))
            ev5v(lambda v: v.tensor_scalar(mask1[:], lg[:], m1[:], None, OP.is_equal))
            ev5v(lambda v: v.scalar_tensor_tensor(lg2[:], mask1[:], -1e30, lg[:], OP.mult, OP.add))
            ev5v(lambda v: v.reduce_max(m2[:], lg2[:], axis=mybir.AxisListType.X))
            ev5v(lambda v: v.tensor_scalar(mask2[:], lg2[:], m2[:], None, OP.is_equal))
            e7 = ev5v(lambda v: v.tensor_add(sel[:], mask1[:], mask2[:]))
            ev_ee = sched.event("scalar", lambda s: s.activation(
                ee[:], lg[:], AF.Exp, bias=negm1[:], scale=1.0), waits=[e7])
            ev5v(lambda v: v.tensor_mul(es[:], ee[:], sel[:]), [ev_ee])
            ev5v(lambda v: v.reduce_sum(ssum5[:], es[:], axis=mybir.AxisListType.X))
            ev5v(lambda v: v.reciprocal(srec[:], ssum5[:]))
            w = [ev_wwz]
            if prev_wtr is not None:
                w.append(prev_wtr)
            e11 = ev5v(lambda v: v.tensor_scalar(ww[:, 0:8], es[:], srec[:], None, OP.mult), w)
            prev_wtr = sched.event("tensor", lambda t: t.transpose(
                ptwf[:], ww[:], identf[:]), waits=[e11] + ([ev_wtev[tt - 1]] if tt >= 1 else []))
            ev_wtev[tt] = sched.event("scalar", lambda s, tt=tt: s.copy(
                wTp[:, tt * 128:(tt + 1) * 128], ptwf[:]), waits=[prev_wtr])

        def em_we(t):
            t.matmul(pwe[0][:], e2sb[:], wTp[:, 0:512], start=True, stop=True)
            return t.matmul(pwe[1][:], e2sb[:], wTp[:, 512:1024], start=True, stop=True)
        ev_wem = sched.event("tensor", em_we, waits=[ev_e2] + ev_wtev)

        def em_wee(s):
            s.copy(wexp[:, 0:512], pwe[0][:])
            return s.copy(wexp[:, 512:1024], pwe[1][:])
        ev_wee = sched.event("scalar", em_wee, waits=[ev_wem])

        def em_h(t):
            for c in range(NCHUNK):
                t.matmul(ph[0][:], abf[:, c:c + 1, :].rearrange('p a n -> p (a n)'), xn2T[:, c * 1024:c * 1024 + 512],
                         start=(c == 0), stop=(c == NCHUNK - 1))
                ins = t.matmul(ph[1][:], abf[:, c:c + 1, :].rearrange('p a n -> p (a n)'),
                               xn2T[:, c * 1024 + 512:(c + 1) * 1024],
                               start=(c == 0), stop=(c == NCHUNK - 1))
            return ins
        ev_hm = sched.event("tensor", em_h, waits=[ev_ab] + ev45)

        def em_he(s):
            s.copy(h_sb[:, 0:512], ph[0][:])
            return s.copy(h_sb[:, 512:1024], ph[1][:])
        ev_he = sched.event("scalar", em_he, waits=[ev_hm])
        ev_hw = ev5v(lambda v: v.tensor_tensor(h_wT[:], h_sb[:], wexp[:], OP.mult),
                     [ev_he, ev_wee])

        p5_mid = sched.last()
        p5ps1.close()
        p5m.close()

        p5b = contextlib.ExitStack()
        ws5 = [p5b.enter_context(nc.sbuf_tensor(f"ws5_{i}", [128, 32, 512], BF16, side="right")) for i in range(2)]
        wbstb = [p5b.enter_context(nc.sbuf_tensor(f"wbstb{i}", [128, 512], BF16, side="right")) for i in range(2)]
        x2r5 = [p5b.enter_context(nc.sbuf_tensor(f"x2r5{i}", [128, 512], F32, side="right")) for i in range(2)]
        x3st = [p5b.enter_context(nc.sbuf_tensor(f"x3st{i}", [128, 512], F32, side="right")) for i in range(2)]
        p5ps2 = contextlib.ExitStack()
        NB5 = 4
        y2ps = [p5ps2.enter_context(nc.psum_tensor(f"y2ps{i}", [128, 512], F32)) for i in range(NB5)]

        sched.set_gate(p5_mid)
        wbr = wbT.rearrange("(q p) o -> p q o", p=128)
        ev_w5 = {}
        ev_wb1 = {}
        ev_y2 = {}
        ev_y2ev = {}
        ev_x2r = {}
        ev_outw = {}
        for os_ in range(8):
            w = []
            if os_ >= 2:
                w.append(ev_y2[(os_ - 2, 7)])
            ev_w5[os_] = sched.event(
                "gpsimd",
                lambda g, os_=os_: g.dma_start(
                    ws5[os_ % 2][:], wbr[:, 0:32, os_ * 512:(os_ + 1) * 512]),
                waits=w, inc=16, sem=f"c_wb{os_ % 2}")
            w = []
            if os_ >= 2:
                w.append(ev_y2[(os_ - 2, 7)])
            ev_wb1[os_] = sched.event(
                "gpsimd", lambda g, os_=os_: g.dma_start(
                    wbstb[os_ % 2][:], wbT[4096:4224, os_ * 512:(os_ + 1) * 512]),
                waits=w, inc=16, sem=f"c_wbb{os_ % 2}")
            for tt in range(8):
                j = os_ * 8 + tt
                w = [ev_w5[os_], ev_wb1[os_]]
                if j == 0:
                    w.extend([ev_hw, ev_bb, ev_oc])
                if j >= NB5:
                    w.append(ev_y2ev[((j - NB5) // 8, (j - NB5) % 8)])

                def em_y2(t, os_=os_, tt=tt, j=j):
                    for dc in range(32):
                        t.matmul(y2ps[j % NB5][:],
                                 xn2T[:, dc * 1024 + tt * 128:dc * 1024 + (tt + 1) * 128],
                                 ws5[os_ % 2][:, dc:dc + 1, :].rearrange("p a n -> p (a n)"),
                                 start=(dc == 0), stop=False)
                    t.matmul(y2ps[j % NB5][:], onechunk[:], wbstb[os_ % 2][:],
                             start=False, stop=False)
                    return t.matmul(y2ps[j % NB5][:], h_wT[:, tt * 128:(tt + 1) * 128],
                                    bbf[:, os_ * 512:(os_ + 1) * 512],
                                    start=False, stop=True)
                ev_y2[(os_, tt)] = sched.event("tensor", em_y2, waits=w)

                w = []
                if j >= 2:
                    w.append(ev_y2ev[((j - 2) // 8, (j - 2) % 8)])
                ev_x2r[(os_, tt)] = sched.event(
                    "sync", lambda s, os_=os_, tt=tt: s.dma_start(
                        x2r5[tt % 2][:],
                        x2scr[tt * 128:(tt + 1) * 128, os_ * 512:(os_ + 1) * 512]),
                    waits=w, inc=16, sem=f"c_x2r{tt % 2}")
                w = [ev_y2[(os_, tt)], ev_x2r[(os_, tt)]]
                if j >= 2:
                    w.append(ev_outw[((j - 2) // 8, (j - 2) % 8)])
                ev_y2ev[(os_, tt)] = sched.event(
                    "vector", lambda v, os_=os_, tt=tt, j=j: v.tensor_tensor(
                        x3st[tt % 2][:], y2ps[j % NB5][:], x2r5[tt % 2][:], OP.add),
                    waits=w)
                ev_outw[(os_, tt)] = sched.event(
                    "sync", lambda s, os_=os_, tt=tt: s.dma_start(
                        out_d[tt * 128:(tt + 1) * 128, os_ * 512:(os_ + 1) * 512],
                        x3st[tt % 2][:]),
                    waits=[ev_y2ev[(os_, tt)]], inc=16, sem=f"c_out{tt % 2}")
        p5b.close()
        p5ps2.close()

        totals = sched.finalize()
        stuck = sched.check_deadlock()
        assert stuck is None, f"deadlock: {stuck}"

        semstack = contextlib.ExitStack()
        with semstack:
            sems = {}
            for name in sorted(totals.keys()):
                sems[name] = semstack.enter_context(nc.semaphore(f"sem_{name}"))
            with nc.Block() as block:

                @block.tensor
                def _(t):
                    sched.body("tensor", t, sems)

                @block.scalar
                def _(s):
                    sched.body("scalar", s, sems)

                @block.vector
                def _(v):
                    sched.body("vector", v, sems)

                @block.gpsimd
                def _(g):
                    sched.body("gpsimd", g, sems)

                @block.sync
                def _(s):
                    sched.body("sync", s, sems)
                    for name in sorted(totals.keys()):
                        s.wait_ge(sems[name], totals[name])
    return nc


_NC_CACHE = []


def kernel(L0_attn_x, L0_attn_wq, L0_attn_wk, L0_attn_wv, L0_attn_wo,
           L0_ffn_W_base, L0_ffn_b_base, L0_ffn_W_router, L0_ffn_lora_A,
           L0_ffn_lora_B):
    x = np.ascontiguousarray(np.asarray(L0_attn_x, dtype=np.float32).reshape(T, D))
    wq = np.ascontiguousarray(np.asarray(L0_attn_wq, dtype=np.float32))
    wk = np.asarray(L0_attn_wk, dtype=np.float32)
    wv = np.asarray(L0_attn_wv, dtype=np.float32)
    wo = np.ascontiguousarray(np.asarray(L0_attn_wo, dtype=np.float32))
    wb = np.asarray(L0_ffn_W_base, dtype=np.float32)
    bb = np.asarray(L0_ffn_b_base, dtype=np.float32)
    wr = np.asarray(L0_ffn_W_router, dtype=np.float32)
    la_ = np.asarray(L0_ffn_lora_A, dtype=np.float32)
    lb = np.asarray(L0_ffn_lora_B, dtype=np.float32)

    wkv = np.ascontiguousarray(np.concatenate([wk, wv], axis=1))
    wbT = np.ascontiguousarray(
        np.concatenate([wb.T, bb[None, :], np.zeros((127, D), np.float32)], axis=0))
    aflat = np.ascontiguousarray(la_.reshape(128, D).T)
    bflat = np.ascontiguousarray(lb.transpose(0, 2, 1).reshape(128, D))
    wrt = np.ascontiguousarray(
        np.concatenate([wr.T, np.zeros((D, 120), np.float32)], axis=1))
    ident = np.eye(128, dtype=bf)
    tri = ((np.arange(512)[None, :] % 128) >= np.arange(128)[:, None])
    e2 = np.zeros((128, 128), np.float32)
    for e in range(8):
        e2[e, e * 16:(e + 1) * 16] = 2.0
    e2 = np.ascontiguousarray(e2.astype(bf))

    if not _NC_CACHE:
        _NC_CACHE.append(build_nc())
    nc = _NC_CACHE[0]

    xr = x.reshape(64, 128, D)
    in_maps = []
    for i in range(8):
        xo = np.ascontiguousarray(xr[i::8].reshape(1024, D))
        mm = np.zeros((128, 8, 512), np.float32)
        mm[:, :i, :] = 1.0
        mm[:, i, :] = tri
        mmask = np.ascontiguousarray(mm.reshape(128, 4096).astype(bf))
        in_maps.append({
            "xown": xo, "wq": wq, "wo": wo, "wbT": wbT, "wkv": wkv,
            "aflat": aflat, "bflat": bflat, "wrt": wrt, "identc": ident,
            "mmaskc": mmask, "e2c": e2,
        })

    res = run_bass_kernel_spmd(nc, in_maps, core_ids=list(range(8)))

    out = np.empty((T, D), np.float32)
    for i in range(8):
        oc = np.asarray(res.results[i]["out"])
        for k in range(8):
            out[(8 * k + i) * 128:(8 * k + i + 1) * 128] = oc[k * 128:(k + 1) * 128]
    return out.reshape(1, T, D)


# version 12
# speedup vs baseline: 1.2387x; 1.0782x over previous
import contextlib
import numpy as np
import ml_dtypes
import concourse.bass as bass
import concourse.mybir as mybir
from concourse.bass_utils import run_bass_kernel_spmd
from concourse.alu_op_type import AluOpType as OP

F32 = mybir.dt.float32
BF16 = mybir.dt.bfloat16
AF = mybir.ActivationFunctionType
bf = ml_dtypes.bfloat16

D = 4096
T = 8192
NPOS = 64
NOWN = 8
NSLOT = 8
NGRP = 8
GH = 4
NCHUNK = 32
SCALE = 1.0 / float(np.sqrt(128.0))
SHIFT = -10.0
NU = 10
EPS = 1e-5


class Event:
    __slots__ = ("engine", "emit", "waits", "idx", "inc", "sem")

    def __init__(self, engine, emit, waits, inc, sem):
        self.engine = engine
        self.emit = emit
        self.waits = list(waits)
        self.idx = None
        self.inc = inc
        self.sem = sem


class Sched:
    ENGINES = ("tensor", "scalar", "vector", "gpsimd", "sync")

    def __init__(self):
        self.ev = {e: [] for e in self.ENGINES}
        self.sem_counts = {}
        self.final = False
        self._gate = {e: [] for e in self.ENGINES}

    def set_gate(self, events):
        evs = [e for e in events if e is not None]
        for eng in self.ENGINES:
            self._gate[eng] = list(evs)

    def event(self, engine, emit, waits=(), inc=1, sem=None):
        w = list(waits)
        if self._gate[engine]:
            w.extend(x for x in self._gate[engine]
                     if not (x.engine == engine and x.sem == engine))
            self._gate[engine] = []
        e = Event(engine, emit, w, inc, sem or engine)
        self.ev[engine].append(e)
        return e

    def last(self):
        seen = {}
        for eng in self.ENGINES:
            for e in self.ev[eng]:
                seen[e.sem] = e
        return list(seen.values())

    def finalize(self):
        order = []
        for eng in self.ENGINES:
            order.extend(self.ev[eng])
        counts = {}
        for eng in self.ENGINES:
            for e in self.ev[eng]:
                counts[e.sem] = counts.get(e.sem, 0) + e.inc
                e.idx = counts[e.sem]
        self.sem_counts = counts
        self.final = True
        return dict(counts)

    def body(self, engine, eng_obj, sems):
        assert self.final
        for e in self.ev[engine]:
            for w in e.waits:
                assert w.idx is not None
                eng_obj.wait_ge(sems[w.sem], w.idx)
            ins = e.emit(eng_obj)
            assert ins is not None, f"emit returned None on {engine}"
            ins.then_inc(sems[e.sem], e.inc)

    def check_deadlock(self):
        pc = {e: 0 for e in self.ev}
        counts = {}
        progress = True
        while progress:
            progress = False
            for eng, lst in self.ev.items():
                while pc[eng] < len(lst):
                    e = lst[pc[eng]]
                    if all(counts.get(w.sem, 0) >= w.idx for w in e.waits):
                        counts[e.sem] = counts.get(e.sem, 0) + e.inc
                        pc[eng] += 1
                        progress = True
                    else:
                        break
        stuck = {e: (pc[e], len(self.ev[e])) for e in self.ev
                 if pc[e] < len(self.ev[e])}
        return stuck or None

    def validate_dma_sems(self):
        return None


def pcol(jj):
    return (jj % 8) * 8 + jj // 8



def batches_for(nj):
    return [(j, min(2, nj - j)) for j in range(0, nj, 2)]


def sgroups_for(nj):
    return [(j, min(6, nj - j)) for j in range(0, nj, 6)]


def add_attention(sched, alloc, palloc, kT, V, qT, rawT, mmsb, bias10,
                  ones_sq, qready=None, filler=None, first_kv_waits=()):
    sps = [palloc(f"at_sps{i}", [128, 1024], F32) for i in range(2)]
    ppv = [palloc(f"at_ppv{i}", [128, 512], F32) for i in range(2)]
    psum_s = palloc("at_psum_s", [128, 512], F32)
    uring = alloc("at_uring", [128, NU * 512], BF16)
    sacc = [alloc(f"at_sacc{i}", [128, 512], F32) for i in range(2)]
    tp = [alloc(f"at_tp{i}", [128, 512], BF16) for i in range(3)]
    sden = [alloc(f"at_sden{i}", [128, 512], F32) for i in range(2)]
    sinvb = sden

    def qslice(g, k):
        off = (k * 32 + g * GH) * 128
        return qT[:, off:off + 512]

    def u_t(gj, n=1):
        s = (gj % NU) * 512
        return uring[:, s:s + n * 512]

    st = {"exp_events": [], "pair_ev_by_gj": [], "pv_ev_by_gj": [],
          "vprev": None, "gj": 0, "gb": 0}
    fin = {}

    def ev_t(emit, waits=()):
        return sched.event("tensor", emit, waits)

    def ev_a(emit, waits=()):
        return sched.event("scalar", emit, waits)

    def ev_v(emit, waits=()):
        w = list(waits)
        if st["vprev"] is not None:
            w.append(st["vprev"])
        e = sched.event("vector", emit, w)
        st["vprev"] = e
        return e

    def make_ssum(x, f):
        waits = [f["sg_last"]]
        if x - 1 in fin and "sevac" in fin[x - 1]:
            waits.append(fin[x - 1]["sevac"])
        f["ssum"] = ev_t(lambda t, x=x: t.matmul(
            psum_s[:], ones_sq[:], sacc[x % 2][:], start=True, stop=True), waits)

    def make_sevac(x, f):
        waits = [f["ssum"]]
        if x - 2 in fin and "raw" in fin[x - 2]:
            waits.append(fin[x - 2]["raw"])
        f["sevac"] = ev_a(lambda s, x=x: s.copy(sden[x % 2][:], psum_s[:]), waits)

    def make_recip(x, f):
        waits = [f["sevac"]]
        if x - 2 in fin and "raw" in fin[x - 2]:
            waits.append(fin[x - 2]["raw"])
        f["recip"] = ev_v(lambda v, x=x: v.reciprocal(
            sinvb[x % 2][:], sden[x % 2][:]), waits)

    def make_raw(x, f, g, k):
        waits = [f["recip"], f["p_last"]]
        off = (k * 32 + g * GH) * 128
        f["raw"] = ev_v(lambda v, off=off, x=x: v.tensor_tensor(
            rawT[:, off:off + 512], ppv[x % 2][:], sinvb[x % 2][:], OP.mult),
            waits)

    entries = [(g, k) for g in range(NGRP) for k in range(NSLOT)]
    for ei, (g, k) in enumerate(entries):
        if filler is not None:
            filler()
        nj = 8 * k + 8
        bl = batches_for(nj)
        nb = len(bl)
        sg = sgroups_for(nj)
        gj0 = st["gj"]
        gb0 = st["gb"]
        f = {}
        fin[ei] = f
        fprev = fin.get(ei - 1)
        exp_ev = [None] * nb
        s_ev = [None] * nb
        p_ev = [None] * nb
        me = [None] * 8
        glast = {}
        for gi, (j0, glen) in enumerate(sg):
            glast.setdefault((j0 + glen) // 2 - 1, []).append(gi)
        first_acc = [True]

        def scores_event(bi):
            j0, b = bl[bi]
            gbi = gb0 + bi
            waits = []
            if gbi >= 2:
                waits.append(st["exp_events"][gbi - 2])
            if bi == 0 and k == 0 and qready is not None:
                waits.append(qready(g))
            if ei == 0 and bi == 0:
                waits.extend(first_kv_waits)

            def emit(t, j0=j0, b=b, bi=bi, g=g, k=k):
                for jj in range(j0, j0 + b):
                    c = pcol(jj)
                    ins = t.matmul(
                        sps[bi % 2][:, (jj - j0) * 512:(jj - j0 + 1) * 512],
                        kT[:, c * 128:(c + 1) * 128],
                        qslice(g, k), start=True, stop=True)
                return ins
            s_ev[bi] = ev_t(emit, waits)

        def exp_event(bi):
            j0, b = bl[bi]
            gj = gj0 + j0
            waits = [s_ev[bi]]
            if gj + b - 1 >= NU:
                waits.append(st["pair_ev_by_gj"][gj + b - 1 - NU])
                waits.append(st["pv_ev_by_gj"][gj + b - 1 - NU])

            def emit(s, j0=j0, b=b, bi=bi, gj=gj):
                return s.activation(u_t(gj, b), sps[bi % 2][:, 0:b * 512],
                                    AF.Exp, bias=bias10[:], scale=SCALE)
            e = ev_a(emit, waits)
            exp_ev[bi] = e
            st["exp_events"].append(e)
            for jj in range(j0, j0 + b):
                if jj >= nj - 8:
                    bb = jj - (nj - 8)
                    me[bb] = sched.event(
                        "vector", lambda v_, jj=jj, bb=bb, gj0=gj0:
                        v_.tensor_tensor(u_t(gj0 + jj), u_t(gj0 + jj),
                                         mmsb[:, bb * 512:(bb + 1) * 512],
                                         OP.mult), [e])

        def pv_event(bi):
            j0, b = bl[bi]
            waits = [exp_ev[bi]]
            if bi == 0 and ei >= 2:
                waits.append(fin[ei - 2]["raw"])
            for jj in range(j0, j0 + b):
                if jj >= nj - 8:
                    waits.append(me[jj - (nj - 8)])

            def emit(t, j0=j0, b=b, ei=ei, nj=nj, gj0=gj0):
                for jj in range(j0, j0 + b):
                    c = pcol(jj)
                    ins = t.matmul(ppv[ei % 2][:], V[:, c * 128:(c + 1) * 128],
                                   u_t(gj0 + jj), start=(jj == 0),
                                   stop=(jj == nj - 1))
                return ins
            p_ev[bi] = ev_t(emit, waits)
            for jj in range(j0, j0 + b):
                st["pv_ev_by_gj"].append(p_ev[bi])

        def tree_group(gi):
            j0, glen = sg[gi]
            npair = glen // 2
            pair_evs = []
            for pi in range(npair):
                jj = j0 + 2 * pi
                w = [exp_ev[jj // 2]]
                if jj >= nj - 8:
                    w.append(me[jj - (nj - 8)])
                    w.append(me[jj + 1 - (nj - 8)])
                e = ev_v(lambda v, pi=pi, jj=jj, gj0=gj0: v.tensor_tensor(
                    tp[pi][:], u_t(gj0 + jj), u_t(gj0 + jj + 1), OP.add), w)
                pair_evs.append(e)
                st["pair_ev_by_gj"].append(e)
                st["pair_ev_by_gj"].append(e)
            start_pi = 0
            if first_acc[0]:
                w = []
                if ei - 2 in fin and "ssum" in fin[ei - 2]:
                    w.append(fin[ei - 2]["ssum"])
                if npair >= 2:
                    ev_v(lambda v, ei=ei: v.tensor_tensor(
                        sacc[ei % 2][:], tp[0][:], tp[1][:], OP.add), w)
                    start_pi = 2
                else:
                    ev_v(lambda v, ei=ei: v.tensor_scalar(
                        sacc[ei % 2][:], tp[0][:], 0.0, None, OP.add), w)
                    start_pi = 1
                first_acc[0] = False
            for pi in range(start_pi, npair):
                ev_v(lambda v, pi=pi, ei=ei: v.tensor_tensor(
                    sacc[ei % 2][:], sacc[ei % 2][:], tp[pi][:], OP.add), [])
            f["sg_last"] = st["vprev"]

        for b in range(nb):
            scores_event(b)
            exp_event(b)
            if fprev:
                if b == 0:
                    make_ssum(ei - 1, fprev)
                    make_sevac(ei - 1, fprev)
                elif b == 1:
                    make_recip(ei - 1, fprev)
                elif b == 2:
                    make_raw(ei - 1, fprev, *entries[ei - 1])
            if b >= 2:
                pv_event(b - 2)
            for gi in glast.get(b, []):
                tree_group(gi)
        pv_event(nb - 2)
        pv_event(nb - 1)
        f["p_last"] = p_ev[nb - 1]
        assert len(st["pv_ev_by_gj"]) == gj0 + nj
        assert len(st["pair_ev_by_gj"]) == gj0 + nj
        st["gj"] += nj
        st["gb"] += nb

    last = len(entries) - 1
    f = fin[last]
    g, k = entries[last]
    make_ssum(last, f)
    make_sevac(last, f)
    make_recip(last, f)
    make_raw(last, f, g, k)
    return fin


def build_nc():
    nc = bass.Bass("TRN2", target_bir_lowering=False)
    xown = nc.dram_tensor("xown", [1024, D], F32, kind="ExternalInput").ap()
    wq = nc.dram_tensor("wq", [D, D], F32, kind="ExternalInput").ap()
    wo = nc.dram_tensor("wo", [D, D], F32, kind="ExternalInput").ap()
    wbT = nc.dram_tensor("wbT", [D + 128, D], F32, kind="ExternalInput").ap()
    wkv = nc.dram_tensor("wkv", [D, 256], F32, kind="ExternalInput").ap()
    aflat = nc.dram_tensor("aflat", [D, 128], F32, kind="ExternalInput").ap()
    bflat = nc.dram_tensor("bflat", [128, D], F32, kind="ExternalInput").ap()
    wrt = nc.dram_tensor("wrt", [D, 128], F32, kind="ExternalInput").ap()
    ident_in = nc.dram_tensor("identc", [128, 128], BF16, kind="ExternalInput").ap()
    mmask_in = nc.dram_tensor("mmaskc", [128, 4096], BF16, kind="ExternalInput").ap()
    e2_in = nc.dram_tensor("e2c", [128, 128], BF16, kind="ExternalInput").ap()
    out_d = nc.dram_tensor("out", [1024, D], F32, kind="ExternalOutput").ap()
    x2scr = nc.dram_tensor("x2scr", [1024, D], F32).ap()
    kvout = nc.dram_tensor("kvout", [128, 2048], BF16)
    kvg = nc.dram_tensor("kvg", [1024, 2048], BF16)

    sched = Sched()
    left = contextlib.ExitStack()

    def la(name, shape, dtype):
        return left.enter_context(nc.sbuf_tensor(name, shape, dtype, side="left"))

    with left:
        ident = la("identsb", [128, 128], BF16)
        mmscope = contextlib.ExitStack()
        mmsb = mmscope.enter_context(
            nc.sbuf_tensor("mmsb", [128, 4096], BF16, side="right"))
        bias10 = la("bias10", [128, 1], F32)
        eps_t = la("epst", [128, 1], F32)
        ones_sq = la("ones_sq", [128, 128], F32)
        xnT = la("xnT", [128, NCHUNK * 1024], BF16)
        kvscope = contextlib.ExitStack()
        kT = kvscope.enter_context(nc.sbuf_tensor("kTs", [128, NPOS * 128], BF16, side="left"))
        V = kvscope.enter_context(nc.sbuf_tensor("Vs", [128, NPOS * 128], BF16, side="left"))

        sched.event("sync", lambda s: s.dma_start(ident[:], ident_in), inc=16, sem="c_setup")
        ev_setup = sched.event("sync", lambda s: s.dma_start(mmsb[:], mmask_in), inc=16, sem="c_setup")
        ev_mb = sched.event("gpsimd", lambda g: g.memset(bias10[:], SHIFT))
        ev_me = sched.event("gpsimd", lambda g: g.memset(eps_t[:], EPS))
        ev_mo = sched.event("gpsimd", lambda g: g.memset(ones_sq[:], 1.0))
        consts = [ev_setup, ev_mb, ev_me, ev_mo]
        sched.set_gate(consts)

        p1 = contextlib.ExitStack()

        def r1(name, shape, dtype):
            return p1.enter_context(nc.sbuf_tensor(name, shape, dtype, side="right"))

        xst = [r1(f"xst{i}", [128, D], F32) for i in range(3)]
        xn_bf = [r1(f"xn_bf{i}", [128, D], BF16) for i in range(2)]
        wkvbf = r1("wkvbf", [128, NCHUNK, 256], BF16)
        kTown = r1("kTown", [128, 1024], BF16)
        Vown = r1("Vown", [128, 1024], BF16)
        stats = r1("stats", [128, 48], F32)
        mv = [r1(f"mv{i}", [128, 2], F32) for i in range(4)]
        rstd = [r1(f"rstd{i}", [128, 1], F32) for i in range(4)]
        negmur = [r1(f"negmur{i}", [128, 1], F32) for i in range(4)]
        lnv = [r1(f"lnv{i}", [128, 1], F32) for i in range(4)]

        p1ps = contextlib.ExitStack()
        ptx = p1ps.enter_context(nc.psum_tensor("ptx", [128, D], BF16))
        kps = [p1ps.enter_context(nc.psum_tensor(f"kps{i}", [128, 256], F32)) for i in range(2)]
        vps = [p1ps.enter_context(nc.psum_tensor(f"vps{i}", [128, 128], F32)) for i in range(2)]

        ev_wkv = sched.event(
            "gpsimd",
            lambda g: g.dma_start(wkvbf[:], wkv.rearrange("(c p) h -> p c h", p=128)),
            waits=consts, inc=16, sem="c_wkv")

        ev_x = [None] * NOWN
        ev_apply = [None] * NOWN
        ev_applyA = [None] * NOWN
        ev_tx = [None] * NOWN
        ev_txevac = [None] * NOWN
        ev_kv = [None] * (NOWN // 2)
        ev_kvevac = [None] * (NOWN // 2)
        ev_vmm = [None] * NOWN
        ev_vev = [None] * NOWN
        ev_ag_prev = [None]
        ev_nm_l = [None] * NOWN

        def stageA(p):
            w = [ev_setup] if p < 3 else [ev_apply[p - 3], ev_applyA[p - 3]]
            ev_x[p] = sched.event(
                "sync",
                lambda s, p=p: s.dma_start(xst[p % 3][:], xown[p * 128:(p + 1) * 128, :]),
                waits=w, inc=16, sem=f"c_x{p % 3}")

            def em_stats(v, p=p):
                for c in range(8):
                    ins = v.bn_stats(stats[:, c * 6:(c + 1) * 6],
                                     xst[p % 3][:, c * 512:(c + 1) * 512])
                return ins
            w = [ev_x[p]]
            if ev_ag_prev[0] is not None:
                w.append(ev_ag_prev[0])
            ev_st = sched.event("vector", em_stats, waits=w)
            ev_ag = sched.event("vector",
                                lambda v, p=p: v.bn_aggr(mv[p % 4][:], stats[:]),
                                waits=[ev_st])
            ev_ag_prev[0] = ev_ag
            wln = [ev_ag]
            if p >= 4:
                wln.append(ev_apply[p - 4])
            ev_ln = sched.event("scalar", lambda s, p=p: s.activation(
                lnv[p % 4][:], mv[p % 4][:, 1:2], AF.Ln, bias=eps_t[:], scale=1.0),
                waits=wln)
            ev_rs = sched.event("scalar", lambda s, p=p: s.activation(
                rstd[p % 4][:], lnv[p % 4][:], AF.Exp, bias=0.0, scale=-0.5),
                waits=[ev_ln])
            ev_nm_l[p] = sched.event("vector", lambda v, p=p: v.tensor_scalar(
                negmur[p % 4][:], mv[p % 4][:, 0:1], rstd[p % 4][:], -1.0,
                OP.mult, OP.mult), waits=[ev_rs])

        def stageB(p):
            wap = [ev_nm_l[p]]
            if p >= 2:
                wap.append(ev_tx[p - 2])
            ev_apA = sched.event("scalar", lambda s, p=p: s.activation(
                xn_bf[p % 2][:, 0:2048], xst[p % 3][:, 0:2048], AF.Identity,
                bias=negmur[p % 4][:], scale=rstd[p % 4][:]), waits=wap)
            ev_apB = sched.event("vector", lambda v, p=p: v.tensor_scalar(
                xn_bf[p % 2][:, 2048:4096], xst[p % 3][:, 2048:4096],
                rstd[p % 4][:], negmur[p % 4][:], OP.mult, OP.add), waits=wap)
            ev_apply[p] = ev_apB
            ev_applyA[p] = ev_apA
            wtx = [ev_apA, ev_apB]
            if p >= 1:
                wtx.append(ev_txevac[p - 1][0])
                wtx.append(ev_txevac[p - 1][1])

            def em_tx(t, p=p):
                for c in range(NCHUNK):
                    ins = t.transpose(ptx[:, c * 128:(c + 1) * 128],
                                      xn_bf[p % 2][:, c * 128:(c + 1) * 128], ident[:])
                return ins
            ev_tx[p] = sched.event("tensor", em_tx, waits=wtx)

            wtev = [ev_tx[p]]
            if p // 2 >= 1:
                wtev.append(ev_kv[p // 2 - 1])
            if p >= 2:
                wtev.append(ev_vmm[p - 2])

            def em_teA(s, p=p):
                return s.copy(
                    xnT[:].rearrange("q (c n) -> q c n", c=NCHUNK)[:, 0:16, p * 128:(p + 1) * 128],
                    ptx[:].rearrange("q (c n) -> q c n", c=NCHUNK)[:, 0:16, :])

            def em_teB(v, p=p):
                return v.tensor_copy(
                    xnT[:].rearrange("q (c n) -> q c n", c=NCHUNK)[:, 16:32, p * 128:(p + 1) * 128],
                    ptx[:].rearrange("q (c n) -> q c n", c=NCHUNK)[:, 16:32, :])
            ev_teA = sched.event("scalar", em_teA, waits=wtev)
            ev_teB = sched.event("vector", em_teB, waits=wtev)
            ev_txevac[p] = (ev_teA, ev_teB)

        def stageB2(p):
            q4, r4 = p // 2, p % 2
            ev_teA, ev_teB = ev_txevac[p]
            wv_ = [ev_teA, ev_teB, ev_wkv]
            if p >= 2:
                wv_.append(ev_vev[p - 2])

            def em_v(t, p=p):
                for c in range(NCHUNK):
                    ins = t.matmul(vps[p % 2][:],
                                   xnT[:, c * 1024 + p * 128:c * 1024 + (p + 1) * 128],
                                   wkvbf[:, c:c + 1, 128:256].rearrange('p a n -> p (a n)'),
                                   start=(c == 0), stop=(c == NCHUNK - 1))
                return ins
            ev_vmm[p] = sched.event("tensor", em_v, waits=wv_)

            if r4 == 1:
                wkvq = [e for j in range(2) for e in ev_txevac[2 * q4 + j]] + [ev_wkv]
                if q4 >= 2:
                    wkvq.append(ev_kvevac[q4 - 2])

                def em_kv(t, q4=q4):
                    for c in range(NCHUNK):
                        ins = t.matmul(kps[q4 % 2][:], wkvbf[:, c:c + 1, 0:128].rearrange('p a n -> p (a n)'),
                                       xnT[:, c * 1024 + q4 * 256:c * 1024 + (q4 + 1) * 256],
                                       start=(c == 0), stop=(c == NCHUNK - 1))
                    return ins
                ev_kv[q4] = sched.event("tensor", em_kv, waits=wkvq)

        def stageB3(p):
            q4, r4 = p // 2, p % 2
            ev_vev[p] = sched.event(
                "scalar", lambda s, p=p: s.copy(Vown[:, p * 128:(p + 1) * 128], vps[p % 2][:]),
                waits=[ev_vmm[p]])
            if r4 == 1:
                def em_kvev(s, q4=q4):
                    return s.copy(kTown[:, q4 * 256:(q4 + 1) * 256], kps[q4 % 2][:])
                ev_kvevac[q4] = sched.event("scalar", em_kvev, waits=[ev_kv[q4]])

        for p in range(NOWN):
            stageA(p)
            if p >= 4:
                stageB3(p - 4)
            if p >= 2:
                stageB2(p - 2)
            if p >= 1:
                stageB(p - 1)
        stageB3(NOWN - 4)
        stageB2(NOWN - 2)
        stageB3(NOWN - 3)
        stageB(NOWN - 1)
        stageB2(NOWN - 1)
        stageB3(NOWN - 2)
        stageB3(NOWN - 1)

        p1_tail = sched.last()

        ev_ko = sched.event(
            "sync", lambda s: s.dma_start(kvout.ap()[:, 0:1024], kTown[:]),
            waits=[ev_kvevac[i] for i in range(NOWN // 2)], inc=16, sem="c_kvo")
        ev_vo = sched.event(
            "sync", lambda s: s.dma_start(kvout.ap()[:, 1024:2048], Vown[:]),
            waits=[ev_vev[i] for i in range(NOWN)], inc=16, sem="c_kvo")
        ev_cc = sched.event(
            "gpsimd", lambda g: g.collective_compute(
                "AllGather", mybir.AluOpType.bypass,
                replica_groups=[list(range(8))],
                ins=[kvout.ap().opt()], outs=[kvg.ap().opt()]),
            waits=[ev_ko, ev_vo], inc=1, sem="c_cc")
        ev_kld = sched.event(
            "sync", lambda s: s.dma_start(
                kT[:].rearrange("p (r n) -> p r n", r=8),
                kvg.ap().rearrange("(r d) n -> d r n", d=128)[:, :, 0:1024]),
            waits=[ev_cc], inc=16, sem="c_kld")
        ev_vld = sched.event(
            "sync", lambda s: s.dma_start(
                V[:].rearrange("p (r n) -> p r n", r=8),
                kvg.ap().rearrange("(r t) n -> t r n", t=128)[:, :, 1024:2048]),
            waits=[ev_cc], inc=16, sem="c_vld")
        p1.close()
        p1ps.close()
        qscope = contextlib.ExitStack()
        qT = qscope.enter_context(
            nc.sbuf_tensor("qT", [128, NSLOT * 32 * 128], BF16, side="right"))

        p2 = contextlib.ExitStack()
        wqbf = [p2.enter_context(nc.sbuf_tensor(f"wqbf{i}", [128, NCHUNK, 128], BF16, side="right")) for i in range(2)]
        p2ps = contextlib.ExitStack()
        qps = p2ps.enter_context(nc.psum_tensor("qps", [128, 512], F32))

        sched.set_gate(p1_tail)
        ev_wq = [None] * 32
        ev_qmm = [None] * 32
        ev_qev = [None] * 32
        _qstate = {"half": 0, "prev_evac": None}

        def q_emit_half():
            hid = _qstate["half"]
            if hid >= 64:
                return
            _qstate["half"] = hid + 1
            ot, hh = hid // 2, hid % 2
            if hh == 0:
                w = [] if ot < 2 else [ev_qmm[ot - 2]]
                ev_wq[ot] = sched.event(
                    "gpsimd",
                    lambda g, ot=ot: g.dma_start(
                        wqbf[ot % 2][:],
                        wq.rearrange("(c p) o -> p c o", p=128)[:, :, ot * 128:(ot + 1) * 128]),
                    waits=w, inc=16, sem=f"c_wq{ot % 2}")

            def em_q(t, ot=ot, hh=hh):
                for c in range(NCHUNK):
                    ins = t.matmul(qps[:], wqbf[ot % 2][:, c:c + 1, :].rearrange('p a n -> p (a n)'),
                                   xnT[:, c * 1024 + hh * 512:c * 1024 + (hh + 1) * 512],
                                   start=(c == 0), stop=(c == NCHUNK - 1))
                return ins
            w = [ev_wq[ot]]
            if _qstate["prev_evac"] is not None:
                w.append(_qstate["prev_evac"])
            ev_m = sched.event("tensor", em_q, waits=w)

            def em_qev(s, ot=ot, hh=hh):
                q3 = qT[:].rearrange("p (k n) -> p k n", k=8)
                return s.copy(q3[:, hh * 4:(hh + 1) * 4, ot * 128:(ot + 1) * 128],
                              qps[:].rearrange("p (k n) -> p k n", k=4))
            ev_e = sched.event("scalar", em_qev, waits=[ev_m])
            _qstate["prev_evac"] = ev_e
            if hh == 1:
                ev_qmm[ot] = ev_m
                ev_qev[ot] = ev_e

        for _ in range(16):
            q_emit_half()
        p3 = contextlib.ExitStack()
        p3ps = contextlib.ExitStack()
        def qready(g):
            e = ev_qev[4 * g + 3]
            assert e is not None, f"q outtile {4 * g + 3} not yet emitted"
            return e
        add_attention(
            sched,
            lambda n, s, d: p3.enter_context(nc.sbuf_tensor(n, s, d, side="right")),
            lambda n, s, d: p3ps.enter_context(nc.psum_tensor(n, s, d)),
            kT, V, qT, qT, mmsb, bias10, ones_sq, qready=qready,
            filler=q_emit_half, first_kv_waits=(ev_kld, ev_vld))
        p3_tail = sched.last()
        p3.close()
        p3ps.close()
        p2.close()
        p2ps.close()
        kvscope.close()

        p4 = contextlib.ExitStack()
        ws4 = [xnT[:, i * 16384:(i + 1) * 16384].rearrange("p (q o) -> p q o", q=32)
               for i in range(2)]
        xsst = [p4.enter_context(nc.sbuf_tensor(f"xsst{i}", [128, 512], F32, side="right")) for i in range(2)]
        x2st = [p4.enter_context(nc.sbuf_tensor(f"x2st{i}", [128, 512], F32, side="right")) for i in range(2)]
        p4ps = contextlib.ExitStack()
        NB4 = 4
        yps = [p4ps.enter_context(nc.psum_tensor(f"yps{i}", [128, 512], F32)) for i in range(NB4)]

        sched.set_gate(p3_tail)
        rawT4 = qT[:].rearrange("p (k n) -> p k n", k=8)
        wor = wo.rearrange("(q p) o -> p q o", p=128)
        ev_w4 = {}
        ev_ymm = {}
        ev_yev = {}
        ev_xs = {}
        ev_x2w = {}
        p3t_tensor = [e for e in p3_tail if e.sem == "tensor"]
        for os_ in range(8):
            w = []
            if os_ >= 2:
                w.append(ev_ymm[(os_ - 2, 7)])
            else:
                w.extend(p3t_tensor)
            ev_w4[os_] = sched.event(
                "gpsimd",
                lambda g, os_=os_: g.dma_start(
                    ws4[os_ % 2], wor[:, :, os_ * 512:(os_ + 1) * 512]),
                waits=w, inc=16, sem=f"c_wo{os_ % 2}")
            for tt in range(8):
                j = os_ * 8 + tt
                w = [ev_w4[os_]]
                if j >= NB4:
                    w.append(ev_yev[((j - NB4) // 8, (j - NB4) % 8)])

                def em_y(t, os_=os_, tt=tt, j=j):
                    for ic in range(32):
                        ins = t.matmul(
                            yps[j % NB4][:],
                            rawT4[:, tt, ic * 128:(ic + 1) * 128],
                            xnT[:, (os_ % 2) * 16384 + ic * 512:(os_ % 2) * 16384 + (ic + 1) * 512],
                            start=(ic == 0), stop=(ic == 31))
                    return ins
                ev_ymm[(os_, tt)] = sched.event("tensor", em_y, waits=w)
                w = []
                if j >= 2:
                    w.append(ev_yev[((j - 2) // 8, (j - 2) % 8)])
                ev_xs[(os_, tt)] = sched.event(
                    "sync", lambda s, os_=os_, tt=tt: s.dma_start(
                        xsst[tt % 2][:],
                        xown[tt * 128:(tt + 1) * 128, os_ * 512:(os_ + 1) * 512]),
                    waits=w, inc=16, sem=f"c_xs{tt % 2}")
                w = [ev_ymm[(os_, tt)], ev_xs[(os_, tt)]]
                if j >= 2:
                    w.append(ev_x2w[((j - 2) // 8, (j - 2) % 8)])
                ev_yev[(os_, tt)] = sched.event(
                    "vector", lambda v, os_=os_, tt=tt, j=j: v.tensor_tensor(
                        x2st[tt % 2][:], yps[j % NB4][:], xsst[tt % 2][:], OP.add),
                    waits=w)
                ev_x2w[(os_, tt)] = sched.event(
                    "sync", lambda s, os_=os_, tt=tt: s.dma_start(
                        x2scr[tt * 128:(tt + 1) * 128, os_ * 512:(os_ + 1) * 512],
                        x2st[tt % 2][:]),
                    waits=[ev_yev[(os_, tt)]], inc=16, sem=f"c_x2w{tt % 2}")
        p4_tail = sched.last()
        p4.close()
        p4ps.close()
        qscope.close()
        mmscope.close()
        xn2T = xnT

        p45 = contextlib.ExitStack()

        def r45(name, shape, dtype):
            return p45.enter_context(nc.sbuf_tensor(name, shape, dtype, side="right"))
        x2rd = [r45(f"x2rd{i}", [128, D], F32) for i in range(2)]
        xn2bf = [r45(f"xn2bf{i}", [128, D], BF16) for i in range(2)]
        st2 = r45("st2", [128, 48], F32)
        mv2 = [r45(f"mv2{i}", [128, 2], F32) for i in range(2)]
        rstd2 = [r45(f"rstd2{i}", [128, 1], F32) for i in range(2)]
        negmur2 = [r45(f"negmur2{i}", [128, 1], F32) for i in range(2)]
        lnv2 = [r45(f"lnv2{i}", [128, 1], F32) for i in range(2)]
        p45ps = contextlib.ExitStack()
        ptx2 = [p45ps.enter_context(nc.psum_tensor(f"ptx2{i}", [128, D], BF16)) for i in range(2)]

        sched.set_gate(p4_tail)
        ev45 = []
        apl = {}
        txl = {}
        evacl = {}
        agp = None
        for tt in range(8):
            w = [] if tt < 2 else [apl[tt - 2]]
            ev_r = sched.event("sync", lambda s, tt=tt: s.dma_start(
                x2rd[tt % 2][:], x2scr[tt * 128:(tt + 1) * 128, :]), waits=w,
                inc=16, sem=f"c_x2rd{tt % 2}")

            def em_st2(v, tt=tt):
                for c in range(8):
                    ins = v.bn_stats(st2[:, c * 6:(c + 1) * 6],
                                     x2rd[tt % 2][:, c * 512:(c + 1) * 512])
                return ins
            ev_s = sched.event("vector", em_st2, waits=[ev_r] + ([agp] if agp else []))
            ev_a2 = sched.event("vector", lambda v, tt=tt: v.bn_aggr(mv2[tt % 2][:], st2[:]),
                                waits=[ev_s])
            agp = ev_a2
            wln = [ev_a2] + ([apl[tt - 2]] if tt >= 2 else [])
            ev_l = sched.event("scalar", lambda s, tt=tt: s.activation(
                lnv2[tt % 2][:], mv2[tt % 2][:, 1:2], AF.Ln, bias=eps_t[:], scale=1.0),
                waits=wln)
            ev_rs2 = sched.event("scalar", lambda s, tt=tt: s.activation(
                rstd2[tt % 2][:], lnv2[tt % 2][:], AF.Exp, bias=0.0, scale=-0.5),
                waits=[ev_l])
            ev_n = sched.event("vector", lambda v, tt=tt: v.tensor_scalar(
                negmur2[tt % 2][:], mv2[tt % 2][:, 0:1], rstd2[tt % 2][:], -1.0,
                OP.mult, OP.mult), waits=[ev_rs2])
            wap = [ev_n] + ([txl[tt - 2]] if tt >= 2 else [])
            ev_ap = sched.event("scalar", lambda s, tt=tt: s.activation(
                xn2bf[tt % 2][:], x2rd[tt % 2][:], AF.Identity,
                bias=negmur2[tt % 2][:], scale=rstd2[tt % 2][:]), waits=wap)
            apl[tt] = ev_ap

            def em_tx2(t, tt=tt):
                for c in range(NCHUNK):
                    ins = t.transpose(ptx2[tt % 2][:, c * 128:(c + 1) * 128],
                                      xn2bf[tt % 2][:, c * 128:(c + 1) * 128], ident[:])
                return ins
            wtx = [ev_ap] + ([evacl[tt - 2][0], evacl[tt - 2][1]] if tt >= 2 else [])
            ev_t45 = sched.event("tensor", em_tx2, waits=wtx)
            txl[tt] = ev_t45

            def em_e2A(s, tt=tt):
                return s.copy(
                    xn2T[:].rearrange("q (c n) -> q c n", c=NCHUNK)[:, 0:16, tt * 128:(tt + 1) * 128],
                    ptx2[tt % 2][:].rearrange("q (c n) -> q c n", c=NCHUNK)[:, 0:16, :])

            def em_e2B(v, tt=tt):
                return v.tensor_copy(
                    xn2T[:].rearrange("q (c n) -> q c n", c=NCHUNK)[:, 16:32, tt * 128:(tt + 1) * 128],
                    ptx2[tt % 2][:].rearrange("q (c n) -> q c n", c=NCHUNK)[:, 16:32, :])
            wal = [ev_t45] + ([ev_ymm[(7, 7)]] if tt == 0 else [])
            eA = sched.event("scalar", em_e2A, waits=wal)
            eB = sched.event("vector", em_e2B, waits=wal)
            evacl[tt] = (eA, eB)
            ev45.extend([eA, eB])
        p45_tail = sched.last()
        p45.close()
        p45ps.close()

        p5 = contextlib.ExitStack()
        p5m = contextlib.ExitStack()

        def a5(name, shape, dtype):
            return p5.enter_context(nc.sbuf_tensor(name, shape, dtype, side="right"))

        def a5m(name, shape, dtype):
            return p5m.enter_context(nc.sbuf_tensor(name, shape, dtype, side="right"))
        bbf = a5("bbf", [128, D], BF16)
        h_wT = a5("h_wT", [128, 1024], BF16)
        onechunk = a5("onechunk", [128, 128], BF16)
        identf = a5m("identf", [128, 128], F32)
        e2sb = a5m("e2sb", [128, 128], BF16)
        abf = a5m("abf", [128, NCHUNK, 128], BF16)
        wrbf = a5m("wrbf", [128, NCHUNK, 128], BF16)
        lgT = a5m("lgT", [128, 1024], F32)
        lg = a5m("lg", [128, 8], F32)
        m1 = a5m("m1", [128, 1], F32)
        negm1 = a5m("negm1", [128, 1], F32)
        ee = a5m("ee", [128, 8], F32)
        mask1 = a5m("mask1", [128, 8], F32)
        lg2 = a5m("lg2", [128, 8], F32)
        m2 = a5m("m2", [128, 1], F32)
        mask2 = a5m("mask2", [128, 8], F32)
        sel = a5m("sel", [128, 8], F32)
        es = a5m("es", [128, 8], F32)
        ssum5 = a5m("ssum5", [128, 1], F32)
        srec = a5m("srec", [128, 1], F32)
        ww = a5m("ww", [128, 128], F32)
        wTp = a5m("wTp", [128, 1024], BF16)
        wexp = a5m("wexp", [128, 1024], BF16)
        h_sb = a5m("h_sb", [128, 1024], BF16)

        p5ps1 = contextlib.ExitStack()
        plg = [p5ps1.enter_context(nc.psum_tensor(f"plg{i}", [128, 512], F32)) for i in range(2)]
        ptr5 = p5ps1.enter_context(nc.psum_tensor("ptr5", [128, 128], F32))
        ptwf = p5ps1.enter_context(nc.psum_tensor("ptwf", [128, 128], F32))
        pwe = [p5ps1.enter_context(nc.psum_tensor(f"pwe{i}", [128, 512], F32)) for i in range(2)]
        ph = [p5ps1.enter_context(nc.psum_tensor(f"ph{i}", [128, 512], F32)) for i in range(2)]

        sched.set_gate(p45_tail)
        ev_ab = sched.event("gpsimd", lambda g: g.dma_start(
            abf[:], aflat.rearrange("(c p) e -> p c e", p=128)), inc=16, sem="c_la")
        ev_bb = sched.event("gpsimd", lambda g: g.dma_start(bbf[:], bflat),
                            inc=16, sem="c_lb")
        ev_wr = sched.event("gpsimd", lambda g: g.dma_start(
            wrbf[:], wrt.rearrange("(c p) e -> p c e", p=128)), inc=16, sem="c_lr")
        ev_e2 = sched.event("gpsimd", lambda g: g.dma_start(e2sb[:], e2_in),
                            inc=16, sem="c_le")
        ev_wwz = sched.event("gpsimd", lambda g: g.memset(ww[:], 0.0))
        ev_idf = sched.event("scalar", lambda s: s.copy(identf[:], ident[:]))
        ev_oc0 = sched.event("gpsimd", lambda g: g.memset(onechunk[:], 0.0))
        ev_oc = sched.event("gpsimd", lambda g: g.memset(onechunk[0:1, :], 1.0),
                            waits=[ev_oc0])

        def em_lg(t):
            for c in range(NCHUNK):
                t.matmul(plg[0][:], wrbf[:, c:c + 1, :].rearrange('p a n -> p (a n)'), xn2T[:, c * 1024:c * 1024 + 512],
                         start=(c == 0), stop=(c == NCHUNK - 1))
                ins = t.matmul(plg[1][:], wrbf[:, c:c + 1, :].rearrange('p a n -> p (a n)'),
                               xn2T[:, c * 1024 + 512:(c + 1) * 1024],
                               start=(c == 0), stop=(c == NCHUNK - 1))
            return ins
        ev_lgm = sched.event("tensor", em_lg, waits=[ev_wr] + ev45)

        def em_lge(s):
            s.copy(lgT[:, 0:512], plg[0][:])
            return s.copy(lgT[:, 512:1024], plg[1][:])
        ev_lge = sched.event("scalar", em_lge, waits=[ev_lgm])

        vch = [None]

        def ev5v(emit, waits=()):
            w = list(waits)
            if vch[0] is not None:
                w.append(vch[0])
            e = sched.event("vector", emit, w)
            vch[0] = e
            return e

        ev_wtev = [None] * 8
        prev_lgx = None
        prev_wtr = None
        for tt in range(8):
            w = [ev_lge]
            if tt == 0:
                w.append(ev_idf)
            if prev_lgx is not None:
                w.append(prev_lgx)
            ev_ltr = sched.event("tensor", lambda t, tt=tt: t.transpose(
                ptr5[:], lgT[:, tt * 128:(tt + 1) * 128], identf[:]), waits=w)
            prev_lgx = sched.event("scalar", lambda s: s.copy(lg[:], ptr5[:, 0:8]),
                                   waits=[ev_ltr])
            ev5v(lambda v: v.reduce_max(m1[:], lg[:], axis=mybir.AxisListType.X), [prev_lgx])
            ev5v(lambda v: v.tensor_scalar(negm1[:], m1[:], -1.0, None, OP.mult))
            ev5v(lambda v: v.tensor_scalar(mask1[:], lg[:], m1[:], None, OP.is_equal))
            ev5v(lambda v: v.scalar_tensor_tensor(lg2[:], mask1[:], -1e30, lg[:], OP.mult, OP.add))
            ev5v(lambda v: v.reduce_max(m2[:], lg2[:], axis=mybir.AxisListType.X))
            ev5v(lambda v: v.tensor_scalar(mask2[:], lg2[:], m2[:], None, OP.is_equal))
            e7 = ev5v(lambda v: v.tensor_add(sel[:], mask1[:], mask2[:]))
            ev_ee = sched.event("scalar", lambda s: s.activation(
                ee[:], lg[:], AF.Exp, bias=negm1[:], scale=1.0), waits=[e7])
            ev5v(lambda v: v.tensor_mul(es[:], ee[:], sel[:]), [ev_ee])
            ev5v(lambda v: v.reduce_sum(ssum5[:], es[:], axis=mybir.AxisListType.X))
            ev5v(lambda v: v.reciprocal(srec[:], ssum5[:]))
            w = [ev_wwz]
            if prev_wtr is not None:
                w.append(prev_wtr)
            e11 = ev5v(lambda v: v.tensor_scalar(ww[:, 0:8], es[:], srec[:], None, OP.mult), w)
            prev_wtr = sched.event("tensor", lambda t: t.transpose(
                ptwf[:], ww[:], identf[:]), waits=[e11] + ([ev_wtev[tt - 1]] if tt >= 1 else []))
            ev_wtev[tt] = sched.event("scalar", lambda s, tt=tt: s.copy(
                wTp[:, tt * 128:(tt + 1) * 128], ptwf[:]), waits=[prev_wtr])

        def em_we(t):
            t.matmul(pwe[0][:], e2sb[:], wTp[:, 0:512], start=True, stop=True)
            return t.matmul(pwe[1][:], e2sb[:], wTp[:, 512:1024], start=True, stop=True)
        ev_wem = sched.event("tensor", em_we, waits=[ev_e2] + ev_wtev)

        def em_wee(s):
            s.copy(wexp[:, 0:512], pwe[0][:])
            return s.copy(wexp[:, 512:1024], pwe[1][:])
        ev_wee = sched.event("scalar", em_wee, waits=[ev_wem])

        def em_h(t):
            for c in range(NCHUNK):
                t.matmul(ph[0][:], abf[:, c:c + 1, :].rearrange('p a n -> p (a n)'), xn2T[:, c * 1024:c * 1024 + 512],
                         start=(c == 0), stop=(c == NCHUNK - 1))
                ins = t.matmul(ph[1][:], abf[:, c:c + 1, :].rearrange('p a n -> p (a n)'),
                               xn2T[:, c * 1024 + 512:(c + 1) * 1024],
                               start=(c == 0), stop=(c == NCHUNK - 1))
            return ins
        ev_hm = sched.event("tensor", em_h, waits=[ev_ab] + ev45)

        def em_he(s):
            s.copy(h_sb[:, 0:512], ph[0][:])
            return s.copy(h_sb[:, 512:1024], ph[1][:])
        ev_he = sched.event("scalar", em_he, waits=[ev_hm])
        ev_hw = ev5v(lambda v: v.tensor_tensor(h_wT[:], h_sb[:], wexp[:], OP.mult),
                     [ev_he, ev_wee])

        p5_mid = sched.last()
        p5ps1.close()
        p5m.close()

        p5b = contextlib.ExitStack()
        ws5 = [p5b.enter_context(nc.sbuf_tensor(f"ws5_{i}", [128, 32, 512], BF16, side="right")) for i in range(2)]
        wbstb = [p5b.enter_context(nc.sbuf_tensor(f"wbstb{i}", [128, 512], BF16, side="right")) for i in range(2)]
        x2r5 = [p5b.enter_context(nc.sbuf_tensor(f"x2r5{i}", [128, 512], F32, side="right")) for i in range(2)]
        x3st = [p5b.enter_context(nc.sbuf_tensor(f"x3st{i}", [128, 512], F32, side="right")) for i in range(2)]
        p5ps2 = contextlib.ExitStack()
        NB5 = 4
        y2ps = [p5ps2.enter_context(nc.psum_tensor(f"y2ps{i}", [128, 512], F32)) for i in range(NB5)]

        sched.set_gate(p5_mid)
        wbr = wbT.rearrange("(q p) o -> p q o", p=128)
        ev_w5 = {}
        ev_wb1 = {}
        ev_y2 = {}
        ev_y2ev = {}
        ev_x2r = {}
        ev_outw = {}
        for os_ in range(8):
            w = []
            if os_ >= 2:
                w.append(ev_y2[(os_ - 2, 7)])
            ev_w5[os_] = sched.event(
                "gpsimd",
                lambda g, os_=os_: g.dma_start(
                    ws5[os_ % 2][:], wbr[:, 0:32, os_ * 512:(os_ + 1) * 512]),
                waits=w, inc=16, sem=f"c_wb{os_ % 2}")
            w = []
            if os_ >= 2:
                w.append(ev_y2[(os_ - 2, 7)])
            ev_wb1[os_] = sched.event(
                "gpsimd", lambda g, os_=os_: g.dma_start(
                    wbstb[os_ % 2][:], wbT[4096:4224, os_ * 512:(os_ + 1) * 512]),
                waits=w, inc=16, sem=f"c_wbb{os_ % 2}")
            for tt in range(8):
                j = os_ * 8 + tt
                w = [ev_w5[os_], ev_wb1[os_]]
                if j == 0:
                    w.extend([ev_hw, ev_bb, ev_oc])
                if j >= NB5:
                    w.append(ev_y2ev[((j - NB5) // 8, (j - NB5) % 8)])

                def em_y2(t, os_=os_, tt=tt, j=j):
                    for dc in range(32):
                        t.matmul(y2ps[j % NB5][:],
                                 xn2T[:, dc * 1024 + tt * 128:dc * 1024 + (tt + 1) * 128],
                                 ws5[os_ % 2][:, dc:dc + 1, :].rearrange("p a n -> p (a n)"),
                                 start=(dc == 0), stop=False)
                    t.matmul(y2ps[j % NB5][:], onechunk[:], wbstb[os_ % 2][:],
                             start=False, stop=False)
                    return t.matmul(y2ps[j % NB5][:], h_wT[:, tt * 128:(tt + 1) * 128],
                                    bbf[:, os_ * 512:(os_ + 1) * 512],
                                    start=False, stop=True)
                ev_y2[(os_, tt)] = sched.event("tensor", em_y2, waits=w)

                w = []
                if j >= 2:
                    w.append(ev_y2ev[((j - 2) // 8, (j - 2) % 8)])
                ev_x2r[(os_, tt)] = sched.event(
                    "sync", lambda s, os_=os_, tt=tt: s.dma_start(
                        x2r5[tt % 2][:],
                        x2scr[tt * 128:(tt + 1) * 128, os_ * 512:(os_ + 1) * 512]),
                    waits=w, inc=16, sem=f"c_x2r{tt % 2}")
                w = [ev_y2[(os_, tt)], ev_x2r[(os_, tt)]]
                if j >= 2:
                    w.append(ev_outw[((j - 2) // 8, (j - 2) % 8)])
                ev_y2ev[(os_, tt)] = sched.event(
                    "vector", lambda v, os_=os_, tt=tt, j=j: v.tensor_tensor(
                        x3st[tt % 2][:], y2ps[j % NB5][:], x2r5[tt % 2][:], OP.add),
                    waits=w)
                ev_outw[(os_, tt)] = sched.event(
                    "sync", lambda s, os_=os_, tt=tt: s.dma_start(
                        out_d[tt * 128:(tt + 1) * 128, os_ * 512:(os_ + 1) * 512],
                        x3st[tt % 2][:]),
                    waits=[ev_y2ev[(os_, tt)]], inc=16, sem=f"c_out{tt % 2}")
        p5b.close()
        p5ps2.close()

        totals = sched.finalize()
        stuck = sched.check_deadlock()
        assert stuck is None, f"deadlock: {stuck}"

        semstack = contextlib.ExitStack()
        with semstack:
            sems = {}
            for name in sorted(totals.keys()):
                sems[name] = semstack.enter_context(nc.semaphore(f"sem_{name}"))
            with nc.Block() as block:

                @block.tensor
                def _(t):
                    sched.body("tensor", t, sems)

                @block.scalar
                def _(s):
                    sched.body("scalar", s, sems)

                @block.vector
                def _(v):
                    sched.body("vector", v, sems)

                @block.gpsimd
                def _(g):
                    sched.body("gpsimd", g, sems)

                @block.sync
                def _(s):
                    sched.body("sync", s, sems)
                    for name in sorted(totals.keys()):
                        s.wait_ge(sems[name], totals[name])
    return nc


_NC_CACHE = []


def kernel(L0_attn_x, L0_attn_wq, L0_attn_wk, L0_attn_wv, L0_attn_wo,
           L0_ffn_W_base, L0_ffn_b_base, L0_ffn_W_router, L0_ffn_lora_A,
           L0_ffn_lora_B):
    x = np.ascontiguousarray(np.asarray(L0_attn_x, dtype=np.float32).reshape(T, D))
    wq = np.ascontiguousarray(np.asarray(L0_attn_wq, dtype=np.float32))
    wk = np.asarray(L0_attn_wk, dtype=np.float32)
    wv = np.asarray(L0_attn_wv, dtype=np.float32)
    wo = np.ascontiguousarray(np.asarray(L0_attn_wo, dtype=np.float32))
    wb = np.asarray(L0_ffn_W_base, dtype=np.float32)
    bb = np.asarray(L0_ffn_b_base, dtype=np.float32)
    wr = np.asarray(L0_ffn_W_router, dtype=np.float32)
    la_ = np.asarray(L0_ffn_lora_A, dtype=np.float32)
    lb = np.asarray(L0_ffn_lora_B, dtype=np.float32)

    wkv = np.ascontiguousarray(np.concatenate([wk, wv], axis=1))
    wbT = np.ascontiguousarray(
        np.concatenate([wb.T, bb[None, :], np.zeros((127, D), np.float32)], axis=0))
    aflat = np.ascontiguousarray(la_.reshape(128, D).T)
    bflat = np.ascontiguousarray(lb.transpose(0, 2, 1).reshape(128, D))
    wrt = np.ascontiguousarray(
        np.concatenate([wr.T, np.zeros((D, 120), np.float32)], axis=1))
    ident = np.eye(128, dtype=bf)
    tri = ((np.arange(512)[None, :] % 128) >= np.arange(128)[:, None])
    e2 = np.zeros((128, 128), np.float32)
    for e in range(8):
        e2[e, e * 16:(e + 1) * 16] = 2.0
    e2 = np.ascontiguousarray(e2.astype(bf))

    if not _NC_CACHE:
        _NC_CACHE.append(build_nc())
    nc = _NC_CACHE[0]

    xr = x.reshape(64, 128, D)
    in_maps = []
    for i in range(8):
        xo = np.ascontiguousarray(xr[i::8].reshape(1024, D))
        mm = np.zeros((128, 8, 512), np.float32)
        mm[:, :i, :] = 1.0
        mm[:, i, :] = tri
        mmask = np.ascontiguousarray(mm.reshape(128, 4096).astype(bf))
        in_maps.append({
            "xown": xo, "wq": wq, "wo": wo, "wbT": wbT, "wkv": wkv,
            "aflat": aflat, "bflat": bflat, "wrt": wrt, "identc": ident,
            "mmaskc": mmask, "e2c": e2,
        })

    res = run_bass_kernel_spmd(nc, in_maps, core_ids=list(range(8)))

    out = np.empty((T, D), np.float32)
    for i in range(8):
        oc = np.asarray(res.results[i]["out"])
        for k in range(8):
            out[(8 * k + i) * 128:(8 * k + i + 1) * 128] = oc[k * 128:(k + 1) * 128]
    return out.reshape(1, T, D)
